# version 12 (final)
# baseline (speedup 1.0000x reference)
import contextlib
import numpy as np
import ml_dtypes

import concourse.bass as bass
import concourse.mybir as mybir
from concourse.bass_utils import run_bass_kernel_spmd

F32 = mybir.dt.float32
BF16 = mybir.dt.bfloat16
I32 = mybir.dt.int32
AL = mybir.AluOpType
ACTF = mybir.ActivationFunctionType
AXX = mybir.AxisListType.X
BF = ml_dtypes.bfloat16

B, S, C, H, HKV, D = 4, 4096, 2048, 16, 4, 128
FF = 8192
NCH = C // 128
SLOTS = 4
PAD = [8, 16, 24, 32]
QCS = [[0, 3, 4, 7], [1, 2, 5, 6]]
SCALE = float(np.float32(1.0) / np.float32(np.sqrt(np.float32(D))))
EPS = 1e-5
NGU = float(2 * FF * C)
ND = float(C * FF)

ENGS = ("sync", "gpsimd", "tensor", "vector", "scalar")


class Sem:
    __slots__ = ("h", "n")

    def __init__(self, h):
        self.h = h
        self.n = 0


class Walk:
    def __init__(self, dv=None):
        self.ops = {e: [] for e in ENGS}
        self.dv = dv
        self._dvb = 0

    def emit(self, eng, fn, waits=(), inc=None, k=1):
        if eng == "vector" and self.dv is not None:
            if inc is None:
                inc = self.dv
                k = 1
            if self._dvb > 0:
                waits = list(waits) + [(self.dv, self._dvb)]
                self._dvb = 0
        if inc is not None:
            inc.n += k
        self.ops[eng].append((fn, tuple(waits), inc, k))
        return inc.n if inc is not None else 0

    def dbar(self):
        if self.dv is not None:
            self._dvb = self.dv.n

    def wait(self, eng, waits):
        self.ops[eng].append((None, tuple(waits), None, 0))

    def fence(self, sems, engines=ENGS):
        waits = tuple((s, s.n) for s in sems if s.n > 0)
        for e in engines:
            self.ops[e].append((None, waits, None, 0))


def _replay(block, w):
    def mk(name):
        def body(eng):
            for fn, waits, inc, k in w.ops[name]:
                for sem, val in waits:
                    if val > 0:
                        eng.wait_ge(sem.h, val)
                if fn is None:
                    continue
                inst = fn(eng)
                if inc is not None:
                    inst.then_inc(inc.h, k)
        return body

    for name in ENGS:
        if w.ops[name]:
            getattr(block, name)(mk(name))


def build():
    nc = bass.Bass("TRN2", target_bir_lowering=False)
    dtt = lambda n, shp, d, k: nc.dram_tensor(n, shp, d, kind=k).ap()

    x_kv = dtt("x_kv", [S, C], F32, "ExternalInput")
    x_own = dtt("x_own", [2048, C], F32, "ExternalInput")
    xT_own = dtt("xT_own", [C, 2048], F32, "ExternalInput")
    cosF = dtt("cosF", [D, S], F32, "ExternalInput")
    sinF = dtt("sinF", [D, S], F32, "ExternalInput")
    cosO = dtt("cosO", [D, 2048], F32, "ExternalInput")
    sinO = dtt("sinO", [D, 2048], F32, "ExternalInput")
    cvec = dtt("cvec", [1, C], F32, "ExternalInput")
    wkT = dtt("wkT", [128, NCH * 512], F32, "ExternalInput")
    wvT = dtt("wvT", [128, NCH * 512], F32, "ExternalInput")
    wgT = dtt("wgT", [128, NCH * 32], F32, "ExternalInput")
    wq_p = dtt("wq_p", [H, 128, NCH * 128], F32, "ExternalInput")
    wo_p = dtt("wo_p", [NCH, 128, NCH * 128], F32, "ExternalInput")
    wmod = dtt("wmod", [48, 128, NCH * 256], F32, "ExternalInput")
    wgu_pr = dtt("wgu_pr", [16, 128, NCH * 1024], F32, "ExternalInput")
    wd_p = dtt("wd_p", [NCH, 128, 64 * 128], F32, "ExternalInput")
    masks = dtt("masks", [SLOTS, 8, 128, 512], BF16, "ExternalInput")
    identb = dtt("identb", [128, 128], BF16, "ExternalInput")
    identf = dtt("identf", [128, 128], F32, "ExternalInput")
    rotTc = dtt("rotTc", [128, 128], BF16, "ExternalInput")
    oneh = dtt("oneh", [128, 256], BF16, "ExternalInput")
    khot = dtt("khot", [128, 64], BF16, "ExternalInput")
    rsel8f = dtt("rsel8f", [8, 1024], F32, "ExternalInput")
    row16 = dtt("row16", [16, 128], BF16, "ExternalInput")
    row1b = dtt("row1b", [1, 128], BF16, "ExternalInput")
    row1f = dtt("row1f", [1, 128], F32, "ExternalInput")
    i16b = dtt("i16b", [16, 16], BF16, "ExternalInput")
    onesf = dtt("onesf", [128, 1], F32, "ExternalInput")

    mid_h = dtt("mid_h", [16, 128, FF], F32, "Internal")
    xnT_h = dtt("xnT_h", [NCH, 128, 2048], F32, "Internal")
    h2q_h = dtt("h2q_h", [16, 128, C], BF16, "Internal")
    trn_h = dtt("trn_h", [NCH, 128, FF], BF16, "Internal")
    outT = dtt("outT", [C, 2048], F32, "ExternalOutput")

    es = contextlib.ExitStack()
    sbuf = lambda n, shp, d: es.enter_context(nc.sbuf_tensor(n, shp, d))
    psum = lambda n, shp, d: es.enter_context(nc.psum_tensor(n, shp, d))

    with es:
        idb = sbuf("idb", [128, 128], BF16)
        idf = sbuf("idf", [128, 128], F32)
        rot_sb = sbuf("rot_sb", [128, 128], BF16)
        oneh_sb = sbuf("oneh_sb", [128, 256], BF16)
        khot_sb = sbuf("khot_sb", [128, 64], BF16)
        rsel_sb = sbuf("rsel_sb", [8, 1024], F32)
        row16_sb = sbuf("row16_sb", [16, 128], BF16)
        row1b_sb = sbuf("row1b_sb", [1, 128], BF16)
        row1f_sb = sbuf("row1f_sb", [1, 128], F32)
        i16_sb = sbuf("i16_sb", [16, 16], BF16)
        onesf_sb = sbuf("onesf_sb", [128, 1], F32)
        sa_t = sbuf("sa_t", [128, 16], F32)
        capa_t = sbuf("capa_t", [128, 16], F32)
        ga_t = sbuf("ga_t", [128, 16], F32)
        gb_t = sbuf("gb_t", [128, 16], F32)
        capb_rep = sbuf("capb_rep", [128, C], BF16)
        sbb_rep = sbuf("sbb_rep", [128, C], BF16)
        kmax2 = sbuf("kmax2", [16, 1], F32)
        invs2_all = sbuf("invs2_all", [128, 16], F32)
        gthr = sbuf("gthr", [128, 6], F32)
        psc_acc = sbuf("psc_acc", [128, 2], F32)
        bnst = sbuf("bnst", [128, 24], F32)
        bnag = sbuf("bnag", [128, 2], F32)
        istd = sbuf("istd", [128, 1], F32)
        bias2 = sbuf("bias2", [128, 1], F32)
        sml = sbuf("sml", [128, 6], F32)
        s16c = sbuf("s16c", [16, 6], F32)
        g8 = sbuf("g8", [8, 1024], BF16)
        gil8 = sbuf("gil8", [8, 512], F32)
        d16 = sbuf("d16", [16, 16], BF16)
        negM = sbuf("negM", [128, 16], F32)
        silT = sbuf("silT", [128, 16], BF16)
        epsc = sbuf("epsc", [128, 1], F32)

        pS2 = psum("pS2", [128, 1024], F32)
        pA = pS2[:, 0:512]
        pB = pS2[:, 512:1024]
        pC = psum("pC", [128, 512], F32)
        pD = psum("pD", [128, 512], F32)
        pE = psum("pE", [16, 512], F32)
        pH = psum("pH", [128, 512], F32)
        pT1 = psum("pT1", [128, 512], BF16)
        pT2 = psum("pT2", [128, 512], BF16)

        semnames = ["PE", "DV", "AC",
                    "CONST", "SX0", "SX1", "SC0", "SC1", "SM", "SXTR", "SMD", "SMD2",
                    "SH2", "SXR0", "SXR1",
                    "GW", "GM0", "GM1", "GPAN0", "GPAN1", "GPS0", "GPS1",
                    "GPS2", "GPS3",
                    "GGU", "GWD", "GT40", "GT41",
                    "STXN", "STH2", "STH2B", "STMD0", "STMD1", "STTRN", "STOUT"]
        sems = {k: Sem(es.enter_context(nc.semaphore(k))) for k in semnames}
        PE, DV, AC = (sems[k] for k in ["PE", "DV", "AC"])
        SX = [sems["SX0"], sems["SX1"]]
        SC = [sems["SC0"], sems["SC1"]]
        GM = [sems["GM0"], sems["GM1"]]
        GPAN = [sems["GPAN0"], sems["GPAN1"]]
        GPS = [sems["GPS0"], sems["GPS1"]]
        GT4 = [sems["GT40"], sems["GT41"]]
        SXR = [sems["SXR0"], sems["SXR1"]]
        STMD = [sems["STMD0"], sems["STMD1"]]
        CONST, SM, SXTR, SMD, SH2 = (sems[k] for k in ["CONST", "SM", "SXTR", "SMD", "SH2"])
        GW, GGU, GWD = (sems[k] for k in ["GW", "GGU", "GWD"])
        STXN, STH2, STTRN, STOUT = (sems[k] for k in ["STXN", "STH2", "STTRN", "STOUT"])
        ALLS = list(sems.values())

        def dma(w, q, dst, src, sem, waits=()):
            return w.emit(q, lambda e, d=dst, s=src: e.dma_start(out=d, in_=s),
                          [x for x in waits if x[1] > 0], inc=sem, k=16)

        def store(w, dst, src, sem, waits=()):
            return w.emit("sync", lambda e, d=dst, s=src: e.dma_start(out=d, in_=s),
                          [x for x in waits if x[1] > 0], inc=sem, k=16)

        def drn(w):
            w.dbar()

        def W_(*xs):
            return [x for x in xs if x[1] > 0]

        def ln_tile(w, xt, tt, waits_x, hT_sb, xnb, ev_waits=(), norm_waits=(),
                    banks=None):
            for u in range(4):
                w.emit("vector", lambda e, u=u, xt=xt: e.bn_stats(
                    bnst[:, 6 * u:6 * u + 6], xt[:, 512 * u:512 * (u + 1)]),
                    waits_x if u == 0 else ())
            drn(w)
            w.emit("vector", lambda e: e.bn_aggr(bnag[:], bnst[:]))
            drn(w)
            nDVst = w.emit("vector", lambda e: e.tensor_copy(sml[:, 0:1], bnag[:, 1:2]),
                           inc=DV)
            nACsq = w.emit("scalar", lambda e: e.activation(
                sml[:, 1:2], sml[:, 0:1], ACTF.Sqrt, bias=epsc[:]),
                [(DV, nDVst)], inc=AC)
            w.emit("vector", lambda e: e.reciprocal(istd[:], sml[:, 1:2]), [(AC, nACsq)])
            drn(w)
            nDVb = w.emit("vector", lambda e: e.scalar_tensor_tensor(
                bias2[:], bnag[:, 0:1], -1.0, istd[:], AL.mult, AL.mult), inc=DV)
            nACn = w.emit("scalar", lambda e, xt=xt, xnb=xnb: e.activation(
                xnb[:], xt[:], ACTF.Identity, bias=bias2[:], scale=istd[:]),
                [(DV, nDVb)] + list(norm_waits), inc=AC)
            if banks is None:
                banks = (pT1, pT2)
            nb = len(banks)
            evg = {}
            for q in range(4):
                pt = banks[q % nb]
                for u in range(4):
                    c = 4 * q + u
                    w.emit("tensor", lambda e, c=c, u=u, pt=pt, xnb=xnb: e.transpose(
                        pt[:, 128 * u:128 * (u + 1)], xnb[:, 128 * c:128 * (c + 1)],
                        idb[:]),
                        (W_((AC, nACn), (DV, DV.n)) if q == 0
                         else W_((DV, evg.get(q - nb, 0)))) if u == 0 else [],
                        inc=PE if u == 3 else None)
                nPEt = PE.n
                for u in range(4):
                    c = 4 * q + u
                    evg[q] = w.emit("vector", lambda e, c=c, u=u, pt=pt, tt=tt, hT_sb=hT_sb: e.tensor_scalar(
                        hT_sb[:, 512 * c + 128 * tt:512 * c + 128 * tt + 128],
                        pt[:, 128 * u:128 * (u + 1)],
                        capa_t[:, c:c + 1], sa_t[:, c:c + 1], AL.mult, AL.add),
                        ([(PE, nPEt)] + list(ev_waits if q == 0 else ()))
                        if u == 0 else [], inc=DV)
            return PE.n, DV.n

        def rope_emit(w, dst_slice, cos_ap, sin_ap, kpre, t1s, t2s, nPEq,
                      pk=None, pr=None, pr_free=0):
            if pk is None:
                pk, pr = pA, pB
            nACk = w.emit("scalar", lambda e, k=kpre, pk=pk: e.copy(k[:], pk[:]),
                          [(PE, nPEq)], inc=AC)
            nPEr = w.emit("tensor", lambda e, k=kpre, pr=pr: e.matmul(
                pr[:], rot_sb[:], k[:], start=True, stop=True),
                W_((AC, nACk), (DV, pr_free)), inc=PE)
            w.emit("vector", lambda e, t1=t1s, ca=cos_ap, pk=pk: e.tensor_tensor(
                t1[:], pk[:], ca, AL.mult), [(AC, nACk)])
            dv_pk = DV.n
            drn(w)
            w.emit("vector", lambda e, t2=t2s, sa=sin_ap, pr=pr: e.tensor_tensor(
                t2[:], pr[:], sa, AL.mult), [(PE, nPEr)])
            dv_pr = DV.n
            drn(w)
            nDVk = w.emit("vector", lambda e, d=dst_slice, t1=t1s, t2=t2s:
                          e.tensor_tensor(d, t1[:], t2[:], AL.add), inc=DV)
            return nDVk, nACk, dv_pk, dv_pr

        with contextlib.ExitStack() as p0s:
            sb0 = lambda n, shp, d: p0s.enter_context(nc.sbuf_tensor(n, shp, d))
            cv_sb = sb0("cv_sb", [1, C], F32)
            silr = sb0("silr", [1, C], F32)
            silb = sb0("silb", [1, C], BF16)
            modrow = sb0("modrow", [1, 6 * C], F32)
            modbf = sb0("modbf", [1, 2 * C], BF16)
            wm_sb = sb0("wm_sb", [128, 2 * 4096], BF16)
            p0scr = sb0("p0scr", [128, 48], F32)
            p0msk = sb0("p0msk", [128, 16], I32)

            w = Walk(DV)
            consts = [(identb, idb), (identf, idf), (rotTc, rot_sb), (oneh, oneh_sb),
                      (khot, khot_sb), (rsel8f, rsel_sb), (row16, row16_sb),
                      (row1b, row1b_sb), (row1f, row1f_sb), (i16b, i16_sb),
                      (onesf, onesf_sb), (cvec, cv_sb)]
            for src, dst in consts:
                dma(w, "sync", dst[:], src[:], CONST)
            nld0 = CONST.n
            w.emit("vector", lambda e: e.memset(kmax2[:], 0.0))
            drn(w)
            w.emit("vector", lambda e: e.memset(epsc[:], float(EPS)))
            drn(w)
            w.emit("vector", lambda e: e.memset(psc_acc[:], 0.0), inc=DV)
            nACs = w.emit("scalar", lambda e: e.activation(silr[:], cv_sb[:], ACTF.Silu),
                          [(CONST, nld0)], inc=AC)
            nDVs = w.emit("vector", lambda e: e.tensor_copy(silb[:], silr[:]),
                          [(AC, nACs)], inc=DV)
            prevsil = 0
            for j in range(16):
                nPEsil = w.emit("tensor", lambda e, j=j: e.transpose(
                    pT1[:, 0:1], silb[0:1, 128 * j:128 * (j + 1)], idb[0:1, 0:1]),
                    W_((DV, nDVs if j == 0 else 0), (DV, prevsil)), inc=PE)
                prevsil = w.emit("vector", lambda e, j=j: e.tensor_copy(
                    silT[:, j:j + 1], pT1[:, 0:1]), [(PE, nPEsil)], inc=DV)
            nDVst = prevsil
            modpe = {}
            modev = {}
            for j in range(48):
                r = j % 2
                bsl = wm_sb[:, 4096 * r:4096 * r + 4096]
                ngp = dma(w, "gpsimd", bsl, wmod[j],
                          GM[r], W_((PE, modpe.get(j - 2, 0))))
                pj = pE if r == 0 else pH
                for c in range(NCH):
                    w.emit("tensor", lambda e, c=c, pj=pj: e.matmul(
                        pj[0:1, 0:256], silT[:, c:c + 1],
                        wm_sb[:, 4096 * (0 if pj is pE else 1) + 256 * c:
                              4096 * (0 if pj is pE else 1) + 256 * (c + 1)][:, :],
                        start=(c == 0), stop=(c == 15)),
                        W_((GM[r], ngp), (DV, nDVst), (DV, modev.get(j - 2, 0)))
                        if c == 0 else [], inc=PE if c == 15 else None)
                modpe[j] = PE.n
                modev[j] = w.emit("vector", lambda e, j=j, pj=pj: e.tensor_copy(
                    modrow[0:1, 256 * j:256 * (j + 1)], pj[0:1, 0:256]),
                    [(PE, modpe[j])], inc=DV)
            drn(w)
            w.emit("vector", lambda e: e.tensor_scalar(
                modbf[0:1, 0:C], modrow[0:1, 4 * C:5 * C], 1.0, None, AL.add),
                [(DV, modev[47])])
            drn(w)
            nDVm = w.emit("vector", lambda e: e.tensor_copy(
                modbf[0:1, C:2 * C], modrow[0:1, 3 * C:4 * C]), inc=DV)
            drn(w)
            prevev = nDVm
            for v in range(2):
                for kk in range(4):
                    nPEr = w.emit("tensor", lambda e, v=v, kk=kk: e.matmul(
                        pH[:], row1b_sb[:],
                        modbf[0:1, C * v + 512 * kk:C * v + 512 * kk + 512],
                        start=True, stop=True), [(DV, prevev)], inc=PE)
                    prevev = w.emit("vector", lambda e, v=v, kk=kk: e.tensor_copy(
                        (capb_rep if v == 0 else sbb_rep)[:, 512 * kk:512 * kk + 512],
                        pH[:]), [(PE, nPEr)], inc=DV)
            for name, vi in [("sa", 0), ("ca", 1), ("ga", 2), ("gb", 5)]:
                for j in range(16):
                    w.emit("tensor", lambda e, vi=vi, j=j: e.transpose(
                        pH[:, j:j + 1],
                        modrow[0:1, C * vi + 128 * j:C * vi + 128 * (j + 1)],
                        idf[0:1, 0:1]), W_((DV, prevev)) if j == 0 else [], inc=PE)
                nPEv = PE.n
                dst = {"sa": sa_t, "ca": capa_t, "ga": ga_t, "gb": gb_t}[name]
                prevev = w.emit("vector", lambda e, dst=dst: e.tensor_copy(
                    dst[:], pH[:, 0:16]), [(PE, nPEv)], inc=DV)
                drn(w)
            w.emit("vector", lambda e: e.tensor_scalar(
                capa_t[:], capa_t[:], 1.0, None, AL.add))
            drn(w)
            for gt in (ga_t, gb_t):
                w.emit("vector", lambda e, gt=gt: e.tensor_scalar(
                    p0msk[:], gt[:], 0.0, None, AL.is_ge))
                drn(w)
                w.emit("vector", lambda e, gt=gt: e.tensor_scalar(
                    p0scr[:, 16:32], gt[:], 0.25, None, AL.max))
                drn(w)
                w.emit("vector", lambda e, gt=gt: e.tensor_scalar(
                    p0scr[:, 32:48], gt[:], -0.25, None, AL.min))
                drn(w)
                w.emit("vector", lambda e, gt=gt: e.select(
                    gt[:], p0msk[:], p0scr[:, 16:32], p0scr[:, 32:48],
                    add_drain=True))
                drn(w)
            w.emit("vector", lambda e: e.tensor_copy(sml[:, 5:6], onesf_sb[:]), inc=DV)
            w.fence(ALLS)
            with nc.Block() as blk:
                _replay(blk, w)

        with contextlib.ExitStack() as p12:
            sb12 = lambda n, shp, d: p12.enter_context(nc.sbuf_tensor(n, shp, d))
            K_sb = sb12("K_sb", [128, HKV * S], BF16)
            V_sb = sb12("V_sb", [128, 32 * 512], BF16)
            hT = sb12("hT", [128, NCH * 512], BF16)
            xnb = sb12("xnb", [128, C], BF16)
            t1s = sb12("t1s", [128, 512], F32)
            t2s = sb12("t2s", [128, 512], F32)
            kpre = sb12("kpre", [128, 512], BF16)
            sqb = sb12("sqb", [128, 512], BF16)
            wg_sb = sb12("wg_sb", [128, NCH * 32], BF16)

            NCHK_GU, NCHK_D = 128, 64

            with contextlib.ExitStack() as p1s:
                sb1 = lambda n, shp, d: p1s.enter_context(nc.sbuf_tensor(n, shp, d))
                wk_sb = sb1("wk_sb", [128, NCH * 512], BF16)
                wv_sb = sb1("wv_sb", [128, NCH * 512], BF16)
                x_t = sb1("x_t", [128, 2 * C], F32)
                cosS = sb1("cosS", [128, 2 * 512], F32)
                sinS = sb1("sinS", [128, 2 * 512], F32)
                psc_buf = sb1("psc_buf", [128, 2 * 2048], BF16)
                psc_f32 = sb1("psc_f32", [128, 2 * 2048], F32)
                psc_dum = sb1("psc_dum", [128, 2048], BF16)
                psc_col8 = sb1("psc_col8", [128, 8], F32)
                bnstB = sb1("bnstB", [128, 24], F32)
                bnagB = sb1("bnagB", [128, 2], F32)
                istdB = sb1("istdB", [128, 1], F32)
                bias2B = sb1("bias2B", [128, 1], F32)
                smlB = sb1("smlB", [128, 6], F32)

                psc = {"i": 0, "acts": {}, "adds": {}}
                PSCQ = [("gpsimd", GPS[0]), ("sync", sems["GPS2"]),
                        ("gpsimd", GPS[1]), ("sync", sems["GPS3"])]

                def psc_step(w):
                    i = psc["i"]
                    if i >= NCHK_GU + NCHK_D:
                        return
                    psc["i"] += 1
                    h = i % 4
                    qname, qsem = PSCQ[h]
                    buf = psc_buf if h % 2 == 0 else psc_f32
                    bsl = buf[:, 2048 * (h // 2):2048 * (h // 2 + 1)]
                    if i < NCHK_GU:
                        src = wgu_pr.rearrange("m p n -> (m p n)")[
                            262144 * i:262144 * (i + 1)].rearrange("(p n) -> p n", p=128)
                        gcol = 0
                    else:
                        q = i - NCHK_GU
                        src = wd_p.rearrange("m p n -> (m p n)")[
                            262144 * q:262144 * (q + 1)].rearrange("(p n) -> p n", p=128)
                        gcol = 1
                    ng = dma(w, qname, bsl, src, qsem,
                             W_((AC, psc["acts"].get(i - 4, 0))))
                    cl = psc_col8[:, i % 8:i % 8 + 1]
                    nac = w.emit("scalar", lambda e, bsl=bsl, cl=cl: e.activation(
                        psc_dum[:], bsl[:], ACTF.Abs, accum_out=cl),
                        W_((qsem, ng), (DV, psc["adds"].get(i - 8, 0)),
                           (AC, psc["acts"].get(i - 1, 0))), inc=AC)
                    psc["acts"][i] = nac
                    psc["adds"][i] = w.emit("vector", lambda e, gcol=gcol, cl=cl: e.tensor_tensor(
                        psc_acc[:, gcol:gcol + 1], psc_acc[:, gcol:gcol + 1],
                        cl, AL.add),
                        W_((AC, nac), (DV, psc["adds"].get(i - 1, 0))), inc=DV)
                w = Walk(DV)
                w.fence(ALLS)
                dma(w, "gpsimd", wk_sb[:], wkT[:], GW)
                dma(w, "gpsimd", wv_sb[:], wvT[:], GW)
                nGPw = dma(w, "gpsimd", wg_sb[:], wgT[:], GW)
                xcons = {}
                knorm_dv = 0
                prev_pe_fin = 0
                kpkf = {}
                kprf = {}

                def ln_stats(par, xt, waits_x):
                    bn, ag, sm = ((bnst, bnag, sml) if par == 0
                                  else (bnstB, bnagB, smlB))
                    isd, b2 = (istd, bias2) if par == 0 else (istdB, bias2B)
                    for u in range(4):
                        w.emit("vector", lambda e, u=u, xt=xt, bn=bn: e.bn_stats(
                            bn[:, 6 * u:6 * u + 6], xt[:, 512 * u:512 * (u + 1)]),
                            waits_x if u == 0 else ())
                    drn(w)
                    w.emit("vector", lambda e, ag=ag, bn=bn: e.bn_aggr(ag[:], bn[:]))
                    drn(w)
                    nDVst = w.emit("vector", lambda e, sm=sm, ag=ag: e.tensor_copy(
                        sm[:, 0:1], ag[:, 1:2]), inc=DV)
                    nACsq = w.emit("scalar", lambda e, sm=sm: e.activation(
                        sm[:, 1:2], sm[:, 0:1], ACTF.Sqrt, bias=epsc[:]),
                        [(DV, nDVst)], inc=AC)
                    w.emit("vector", lambda e, isd=isd, sm=sm: e.reciprocal(
                        isd[:], sm[:, 1:2]), [(AC, nACsq)])
                    drn(w)
                    nDVb = w.emit("vector", lambda e, b2=b2, ag=ag, isd=isd: e.scalar_tensor_tensor(
                        b2[:], ag[:, 0:1], -1.0, isd[:], AL.mult, AL.mult), inc=DV)
                    return nDVb

                def ln_fin(par, xt, tt, nDVb, prev_pe):
                    isd, b2 = (istd, bias2) if par == 0 else (istdB, bias2B)
                    nACn = w.emit("scalar", lambda e, xt=xt, isd=isd, b2=b2: e.activation(
                        xnb[:], xt[:], ACTF.Identity, bias=b2[:], scale=isd[:]),
                        W_((DV, nDVb), (PE, prev_pe)), inc=AC)
                    evg = {}
                    for q in range(4):
                        pt = pT1 if q % 2 == 0 else pT2
                        for u in range(4):
                            c = 4 * q + u
                            w.emit("tensor", lambda e, c=c, u=u, pt=pt: e.transpose(
                                pt[:, 128 * u:128 * (u + 1)],
                                xnb[:, 128 * c:128 * (c + 1)], idb[:]),
                                (W_((AC, nACn), (DV, DV.n)) if q == 0
                                 else W_((DV, evg.get(q - 2, 0)))) if u == 0 else [],
                                inc=PE if u == 3 else None)
                        nPEt = PE.n
                        for u in range(4):
                            c = 4 * q + u
                            evg[q] = w.emit("vector", lambda e, c=c, u=u, pt=pt, tt=tt: e.tensor_scalar(
                                hT[:, 512 * c + 128 * tt:512 * c + 128 * tt + 128],
                                pt[:, 128 * u:128 * (u + 1)],
                                capa_t[:, c:c + 1], sa_t[:, c:c + 1], AL.mult, AL.add),
                                [(PE, nPEt)] if u == 0 else [], inc=DV)
                    return PE.n

                for tc in range(8):
                    dma(w, "sync", cosS[:, 512 * (tc % 2):512 * (tc % 2) + 512],
                        cosF[:, 512 * tc:512 * (tc + 1)], SC[tc % 2],
                        W_((DV, xcons.get(("cs", tc - 2), 0))))
                    nldc = dma(w, "sync", sinS[:, 512 * (tc % 2):512 * (tc % 2) + 512],
                               sinF[:, 512 * tc:512 * (tc + 1)], SC[tc % 2])
                    finp = None
                    for tt in range(4):
                        i = 4 * tc + tt
                        r = i % 2
                        nld = dma(w, "sync", x_t[:, C * r:C * r + C],
                                  x_kv[128 * i:128 * (i + 1), :], SX[r],
                                  W_((DV, xcons.get(i - 2, 0)),
                                     (AC, xcons.get((i - 2, "a"), 0))))
                        xt = x_t[:, C * r:C * r + C]
                        nDVb = ln_stats(r, xt, [(SX[r], nld)])
                        xcons[i] = DV.n
                        psc_step(w)
                        psc_step(w)
                        psc_step(w)
                        if finp is not None:
                            prev_pe_fin = ln_fin(finp[0], finp[1], finp[2],
                                                 finp[3], prev_pe_fin)
                            xcons[(finp[4], "a")] = AC.n
                        psc_step(w)
                        psc_step(w)
                        psc_step(w)
                        finp = (r, xt, tt, nDVb, i)
                    prev_pe_fin = ln_fin(finp[0], finp[1], finp[2],
                                         finp[3], prev_pe_fin)
                    xcons[(finp[4], "a")] = AC.n
                    hT_ready = DV.n
                    for kv in range(4):
                        par = kv % 2
                        pk = pA if par == 0 else pC
                        pr = pB if par == 0 else pD
                        pkf = kpkf.get(par, (0, 0))
                        for c in range(NCH):
                            w.emit("tensor", lambda e, kv=kv, c=c, pk=pk: e.matmul(
                                pk[:], wk_sb[:, 512 * c + 128 * kv:512 * c + 128 * kv + 128],
                                hT[:, 512 * c:512 * (c + 1)],
                                start=(c == 0), stop=(c == 15)),
                                W_((GW, nGPw), (DV, hT_ready if kv == 0 else 0),
                                   (DV, pkf[0]), (AC, AC.n)) if c == 0 else [],
                                inc=PE if c == 15 else None)
                        nPEq = PE.n
                        ksl = K_sb[:, 4096 * kv + 512 * tc:4096 * kv + 512 * tc + 512]
                        w.wait("vector", W_((SC[tc % 2], nldc)))
                        nDVk, nACk, dv_pk, dv_pr = rope_emit(
                            w, ksl,
                            cosS[:, 512 * (tc % 2):512 * (tc % 2) + 512],
                            sinS[:, 512 * (tc % 2):512 * (tc % 2) + 512],
                            kpre, t1s, t2s, nPEq, pk, pr, kprf.get(par, 0))
                        kpkf[par] = (dv_pk, nACk)
                        kprf[par] = dv_pr
                        xcons[("cs", tc)] = nDVk
                        nACq = w.emit("scalar", lambda e, ksl=ksl: e.activation(
                            sqb[:], ksl, ACTF.Square), [(DV, nDVk)], inc=AC)
                        w.emit("tensor", lambda e, kv=kv: e.matmul(
                            pE[:], khot_sb[:, 16 * kv:16 * kv + 16], sqb[:],
                            start=(kv == 0), stop=(kv == 3)),
                            W_((AC, nACq), (DV, knorm_dv)),
                            inc=PE if kv == 3 else None)
                    nPEn = PE.n
                    w.emit("vector", lambda e: e.tensor_reduce(
                        s16c[:, 0:1], pE[:], AXX, AL.max), [(PE, nPEn)])
                    drn(w)
                    knorm_dv = w.emit("vector", lambda e: e.tensor_tensor(
                        kmax2[:], kmax2[:], s16c[:, 0:1], AL.max), inc=DV)
                    drn(w)
                    for tt in range(4):
                        i = 4 * tc + tt
                        pv = pC if tt % 2 == 0 else pD
                        for c in range(NCH):
                            w.emit("tensor", lambda e, c=c, tt=tt, pv=pv: e.matmul(
                                pv[:], hT[:, 512 * c + 128 * tt:512 * c + 128 * tt + 128],
                                wv_sb[:, 512 * c:512 * (c + 1)],
                                start=(c == 0), stop=(c == 15)),
                                W_((AC, AC.n), (DV, DV.n if tt < 2 else 0))
                                if c == 0 else [],
                                inc=PE if c == 15 else None)
                        nPEv = PE.n
                        w.emit("scalar", lambda e, i=i, pv=pv: e.copy(
                            V_sb[:, 512 * i:512 * (i + 1)], pv[:]),
                            [(PE, nPEv)], inc=AC)
                assert psc["i"] == NCHK_GU + NCHK_D, psc["i"]
                w.fence(ALLS)
                with nc.Block() as blk:
                    _replay(blk, w)

            with contextlib.ExitStack() as p2s:
                sb2 = lambda n, shp, d: p2s.enter_context(nc.sbuf_tensor(n, shp, d))
                qT = sb2("qT", [128, 8 * 512], BF16)
                gattn = sb2("gattn", [128, 16 * 512], BF16)
                xo_t = sb2("xo_t", [128, C], F32)
                Pbuf = sb2("Pbuf", [128, 2 * 1024], BF16)
                negM2 = sb2("negM2", [128, 8], F32)
                mask_sb = sb2("mask_sb", [128, 8 * 512], BF16)
                pan_sb = sb2("pan_sb", [128, 2 * 2048], BF16)
                xnt = sb2("xnt", [128, 2 * 512], F32)
                xtr = sb2("xtr", [128, 512], F32)
                cosO2 = sb2("cosO2", [128, 512], F32)
                sinO2 = sb2("sinO2", [128, 512], F32)
                xntok = sb2("xntok", [128, 4 * 2048], BF16)
                h2st = sb2("h2st", [128, 2 * 512], BF16)
                q_s1 = sb2("q_s1", [128, C], F32)
                q2scr = sb2("q2scr", [128, 1024], F32)
                q_ng = sb2("q_ng", [128, 512], BF16)
                q_d = sb2("q_d", [128, 512], I32)
                ln2st = sb2("ln2st", [128, 24], F32)
                ln2ag = sb2("ln2ag", [128, 2], F32)
                ln2istd = sb2("ln2istd", [128, 1], F32)
                ln2b2 = sb2("ln2b2", [128, 1], F32)
                ln2sml = sb2("ln2sml", [128, 6], F32)

                w = Walk(DV)
                w.fence(ALLS)
                araw_free = {hh: 0 for hh in range(8)}
                araw_gmult = {hh: 0 for hh in range(8)}
                panel_cons = {}
                mask_cons = 0
                hT_free_pe = 0
                xnt_store = {}
                bank_exp = {"last": 0}

                h2ev_last = {"v": 0}
                h2st_ring = {0: 0, 1: 0}

                def ln2_gen(s, ready_ac):
                    for tt in range(4):
                        i = 4 * s + tt
                        xsl = xntok[:, 2048 * tt:2048 * (tt + 1)]
                        for u in range(4):
                            w.emit("vector", lambda e, u=u, xsl=xsl: e.bn_stats(
                                ln2st[:, 6 * u:6 * u + 6],
                                xsl[:, 512 * u:512 * (u + 1)]),
                                W_((AC, ready_ac)) if (u == 0 and tt == 0) else ())
                        drn(w)
                        w.emit("vector", lambda e: e.bn_aggr(ln2ag[:], ln2st[:]))
                        drn(w)
                        nDVst = w.emit("vector", lambda e: e.tensor_copy(
                            ln2sml[:, 0:1], ln2ag[:, 1:2]), inc=DV)
                        nACsq = w.emit("scalar", lambda e: e.activation(
                            ln2sml[:, 1:2], ln2sml[:, 0:1], ACTF.Sqrt, bias=epsc[:]),
                            [(DV, nDVst)], inc=AC)
                        yield
                        w.emit("vector", lambda e: e.reciprocal(
                            ln2istd[:], ln2sml[:, 1:2]), [(AC, nACsq)])
                        drn(w)
                        nDVb = w.emit("vector", lambda e: e.scalar_tensor_tensor(
                            ln2b2[:], ln2ag[:, 0:1], -1.0, ln2istd[:],
                            AL.mult, AL.mult), inc=DV)
                        nACn = w.emit("scalar", lambda e, xsl=xsl: e.activation(
                            q_s1[:], xsl, ACTF.Identity, bias=ln2b2[:],
                            scale=ln2istd[:]), [(DV, nDVb)], inc=AC)
                        yield
                        for ch in range(4):
                            w.emit("vector", lambda e, ch=ch: e.tensor_tensor(
                                q2scr[:, 0:512], q_s1[:, 512 * ch:512 * (ch + 1)],
                                capb_rep[:, 512 * ch:512 * (ch + 1)], AL.mult),
                                [(AC, nACn)])
                            drn(w)
                            w.emit("vector", lambda e, ch=ch: e.tensor_tensor(
                                q_s1[:, 512 * ch:512 * (ch + 1)], q2scr[:, 0:512],
                                sbb_rep[:, 512 * ch:512 * (ch + 1)], AL.add))
                            drn(w)
                            if ch == 1:
                                bar1 = DV.n
                                yield
                                w.wait("vector", [(DV, bar1)])
                        w.emit("vector", lambda e: e.tensor_reduce(
                            ln2sml[:, 2:3], q_s1[:], AXX, AL.max,
                            apply_absolute_value=True))
                        drn(w)
                        w.emit("vector", lambda e: e.tensor_scalar(
                            ln2sml[:, 2:3], ln2sml[:, 2:3], float(EPS), None, AL.max))
                        drn(w)
                        w.emit("vector", lambda e, i=i: e.tensor_scalar(
                            invs2_all[:, i:i + 1], ln2sml[:, 2:3], 1.0 / 127.0,
                            None, AL.mult))
                        drn(w)
                        w.emit("vector", lambda e: e.reciprocal(
                            ln2sml[:, 3:4], ln2sml[:, 2:3]))
                        drn(w)
                        nDVs127 = w.emit("vector", lambda e: e.tensor_scalar(
                            ln2sml[:, 4:5], ln2sml[:, 3:4], 127.0, None, AL.mult),
                            inc=DV)
                        yield
                        tp_pe = 0
                        for qc_ in range(4):
                            hsl = q_s1[:, 512 * qc_:512 * (qc_ + 1)]
                            esl = q2scr[:, 0:512]
                            rsl = q2scr[:, 512:1024]
                            w.emit("vector", lambda e, hsl=hsl: e.tensor_scalar(
                                q_d[:], hsl, ln2sml[:, 4:5], None, AL.mult),
                                W_((DV, nDVs127) if qc_ == 0 else (PE, tp_pe)))
                            drn(w)
                            w.emit("vector", lambda e, hsl=hsl, esl=esl: e.scalar_tensor_tensor(
                                esl, hsl, ln2sml[:, 4:5], q_d[:],
                                AL.mult, AL.subtract))
                            drn(w)
                            w.emit("vector", lambda e, esl=esl, rsl=rsl: e.scalar_tensor_tensor(
                                rsl, esl, 0.5, q_d[:], AL.is_gt, AL.add))
                            drn(w)
                            nq = w.emit("vector", lambda e, esl=esl, rsl=rsl: e.scalar_tensor_tensor(
                                q_ng[:], esl, -0.5, rsl, AL.is_lt, AL.subtract),
                                W_((PE, tp_pe)), inc=DV)
                            yield
                            for u in range(4):
                                w.emit("tensor", lambda e, u=u: e.transpose(
                                    pT2[:, 128 * u:128 * (u + 1)],
                                    q_ng[:, 128 * u:128 * (u + 1)], idb[:]),
                                    W_((DV, nq), (AC, h2ev_last["v"]))
                                    if u == 0 else [], inc=PE if u == 3 else None)
                            tp_pe = PE.n
                            hf = qc_ % 2
                            sh2 = STH2 if hf == 0 else sems["STH2B"]
                            h2ev_last["v"] = w.emit("scalar", lambda e, hf=hf: e.copy(
                                h2st[:, 512 * hf:512 * (hf + 1)], pT2[:]),
                                [(PE, tp_pe), (sh2, h2st_ring[hf])], inc=AC)
                            h2st_ring[hf] = store(
                                w, h2q_h[i][:, 512 * qc_:512 * (qc_ + 1)],
                                h2st[:, 512 * hf:512 * (hf + 1)], sh2,
                                [(AC, h2ev_last["v"])])
                            yield

                gen = {"g": None}

                def drip(n=1):
                    g = gen["g"]
                    if g is None:
                        return
                    for _ in range(n):
                        try:
                            next(g)
                        except StopIteration:
                            gen["g"] = None
                            return

                for s in range(SLOTS):
                    nldc2a = dma(w, "sync", cosO2[:], cosO[:, 512 * s:512 * (s + 1)],
                                 SC[0], W_((DV, DV.n)))
                    nldc2 = dma(w, "sync", sinO2[:], sinO[:, 512 * s:512 * (s + 1)],
                                SC[0])
                    nldm = 0
                    for j in range(8):
                        nldm = dma(w, "sync", mask_sb[:, 512 * j:512 * j + 512],
                                   masks[s, j], SM,
                                   W_((DV, mask_cons)) if j == 0 else ())
                    cons_q = {}
                    for tt in range(4):
                        i = 4 * s + tt
                        nld = dma(w, "sync", xo_t[:], x_own[128 * i:128 * (i + 1), :],
                                  SX[0],
                                  W_((DV, cons_q.get(tt - 1, DV.n if tt == 0 else 0)),
                                     (AC, cons_q.get((tt - 1, "a"), 0))))
                        ln_tile(w, xo_t, tt, [(SX[0], nld)], hT, xnb,
                                ev_waits=W_((PE, hT_free_pe if tt == 0 else 0)),
                                banks=(pT1,))
                        cons_q[tt] = DV.n
                        cons_q[(tt, "a")] = AC.n
                        drip(3)
                    nDVh = DV.n
                    gparts = [(0, 0), (16, 1), (8, 2), (24, 3)]
                    gprev = 0
                    for off, part in gparts:
                        for c in range(NCH):
                            w.emit("tensor", lambda e, c=c, off=off: e.matmul(
                                pE[0:8, :], wg_sb[:, 32 * c + off:32 * c + off + 8],
                                hT[:, 512 * c:512 * (c + 1)],
                                start=(c == 0), stop=(c == 15)),
                                W_((DV, nDVh), (AC, gprev)) if c == 0 else [],
                                inc=PE if c == 15 else None)
                        nPEg = PE.n
                        gg = part // 2
                        dst = (g8[:, 512 * gg:512 * gg + 512] if part % 2 == 0
                               else gil8[:])
                        gprev = w.emit("scalar", lambda e, dst=dst: e.activation(
                            dst, pE[0:8, :], ACTF.Sigmoid),
                            W_((PE, nPEg), (DV, DV.n)), inc=AC)
                        if part % 2 == 1:
                            w.emit("vector", lambda e, gg=gg: e.tensor_tensor(
                                g8[:, 512 * gg:512 * gg + 512],
                                g8[:, 512 * gg:512 * gg + 512], gil8[:], AL.add),
                                [(AC, gprev)], inc=DV)
                    nDVgates = DV.n
                    drip(2)
                    for g2 in range(2):
                        qn_prev_dv = DV.n
                        qprep = {}
                        ropeDV = {}
                        rotPE = {}
                        sq_pend = {"v": None}

                        def emit_qn_pend(qn_prev_dv=qn_prev_dv):
                            if sq_pend["v"] is None:
                                return
                            h, nacq = sq_pend["v"]
                            sq_pend["v"] = None
                            w.emit("tensor", lambda e, h=h: e.matmul(
                                pE[:], oneh_sb[:, 16 * h:16 * h + 16], sqb[:],
                                start=(h % 8 == 0), stop=(h % 8 == 7)),
                                W_((AC, nacq),
                                   (DV, qn_prev_dv if h % 8 == 0 else 0)), inc=PE)

                        def emit_qproj(hh, g2=g2):
                            h = 8 * g2 + hh
                            r = h % 2
                            pq = pA if hh % 2 == 0 else pC
                            nGPq = dma(w, "gpsimd", pan_sb[:, 2048 * r:2048 * r + 2048],
                                       wq_p[h], GPAN[r], W_((DV, panel_cons.get(r, 0))))
                            for c in range(NCH):
                                w.emit("tensor", lambda e, c=c, r=r, pq=pq: e.matmul(
                                    pq[:], pan_sb[:, 2048 * r + 128 * c:2048 * r + 128 * c + 128],
                                    hT[:, 512 * c:512 * (c + 1)],
                                    start=(c == 0), stop=(c == 15)),
                                    W_((GPAN[r], nGPq),
                                       (DV, ropeDV.get(hh - 2, DV.n if hh < 2 else 0)),
                                       (AC, AC.n if (hh < 2 and g2 == 0) else 0),
                                       (AC, bank_exp.get("last", 0) if hh < 2 else 0))
                                    if c == 0 else [], inc=PE if c == 15 else None)
                            qprep[hh] = (PE.n, pq, pB if hh % 2 == 0 else pD)

                        def emit_qfin(hh, g2=g2):
                            h = 8 * g2 + hh
                            nPEq, pq, pr = qprep[hh]
                            qsl = qT[:, 512 * hh:512 * hh + 512]
                            nACk = w.emit("scalar", lambda e, pq=pq: e.copy(kpre[:], pq[:]),
                                          W_((PE, nPEq), (PE, rotPE.get(hh - 1, 0))), inc=AC)
                            nPEr = w.emit("tensor", lambda e, pr=pr: e.matmul(
                                pr[:], rot_sb[:], kpre[:], start=True, stop=True),
                                [(AC, nACk)], inc=PE)
                            rotPE[hh] = nPEr
                            emit_qn_pend()
                            peg = PE.n
                            w.wait("vector", W_((SC[0], nldc2)))
                            w.emit("vector", lambda e, pq=pq: e.tensor_tensor(
                                t1s[:], pq[:], cosO2[:], AL.mult), [(AC, nACk)])
                            w.dbar()
                            w.emit("vector", lambda e, pr=pr: e.tensor_tensor(
                                t2s[:], pr[:], sinO2[:], AL.mult), [(PE, nPEr)])
                            w.dbar()
                            nDVq = w.emit("vector", lambda e, qsl=qsl: e.tensor_tensor(
                                qsl, t1s[:], t2s[:], AL.add))
                            ropeDV[hh] = nDVq
                            panel_cons[h % 2] = nDVq
                            nACq = w.emit("scalar", lambda e, qsl=qsl: e.activation(
                                sqb[:], qsl, ACTF.Square),
                                W_((DV, nDVq), (PE, peg)), inc=AC)
                            sq_pend["v"] = (h, nACq)

                        emit_qproj(0)
                        for hh in range(8):
                            if hh < 7:
                                emit_qproj(hh + 1)
                            emit_qfin(hh)
                        emit_qn_pend()
                        nPEqn = PE.n
                        w.emit("vector", lambda e: e.tensor_reduce(
                            s16c[:, 0:1], pE[:], AXX, AL.max), [(PE, nPEqn)])
                        drn(w)
                        w.emit("vector", lambda e: e.tensor_tensor(
                            s16c[:, 1:2], s16c[:, 0:1], kmax2[:], AL.mult))
                        drn(w)
                        nDVn = w.emit("vector", lambda e: e.tensor_copy(
                            s16c[:, 2:3], s16c[:, 1:2]), inc=DV)
                        nACn2 = w.emit("scalar", lambda e: e.activation(
                            s16c[:, 3:4], s16c[:, 2:3], ACTF.Sqrt), [(DV, nDVn)], inc=AC)
                        w.emit("vector", lambda e: e.tensor_scalar(
                            s16c[:, 4:5], s16c[:, 3:4], -SCALE, None, AL.mult),
                            [(AC, nACn2)])
                        drn(w)
                        nDVd = w.emit("vector", lambda e: e.tensor_tensor(
                            d16[:], i16_sb[:], s16c[:, 4:5].to_broadcast([16, 16]),
                            AL.mult), inc=DV)
                        nPEr2 = w.emit("tensor", lambda e: e.matmul(
                            pH[:, 0:16], row16_sb[:], d16[:], start=True, stop=True),
                            [(DV, nDVd)], inc=PE)
                        w.emit("vector", lambda e: e.tensor_copy(
                            negM[:], pH[:, 0:16]), [(PE, nPEr2)])
                        drn(w)
                        nDVng = w.emit("vector", lambda e, g2=g2: e.tensor_tensor(
                            negM2[:, 4 * g2:4 * g2 + 4],
                            negM[:, 8 * g2:8 * g2 + 8:2],
                            negM[:, 8 * g2 + 1:8 * g2 + 8:2], AL.min), inc=DV)
                        drn(w)
                        npad = PAD[s]
                        pv_last = {}
                        slot_cons = {0: 0, 1: 0}
                        prev_exp = 0
                        prev_mask = {}
                        for hh0 in range(0, 8, 2):
                            p2i = (8 * g2 + hh0) // 2
                            pend = None

                            def emit_lpv(pw, sl_, tkb_, g2=g2, hh0=hh0):
                                npad_ = PAD[s]
                                for half in range(2):
                                    hh = hh0 + half
                                    h = 8 * g2 + hh
                                    kv = h // 4
                                    P_sl = Pbuf[:, 1024 * sl_ + 512 * half:
                                                1024 * sl_ + 512 * half + 512]
                                    lh = oneh_sb[:, 16 * h + (8 if h >= 8 else 0):
                                                 16 * h + (8 if h >= 8 else 0) + 8]
                                    w.emit("tensor", lambda e, lh=lh, P_sl=P_sl, hh0=hh0, tkb_=tkb_, half=half, npad_=npad_: e.matmul(
                                        pE[0:8, :], lh, P_sl,
                                        start=(hh0 == 0 and tkb_ == 0 and half == 0),
                                        stop=(hh0 == 6 and tkb_ == npad_ - 1 and half == 1)),
                                        W_(*pw,
                                           (DV, qn_prev_dv if (hh0 == 0 and tkb_ == 0 and half == 0) else 0)))
                                    psAT = pC if half == 0 else pD
                                    w.emit("tensor", lambda e, psAT=psAT, tkb_=tkb_, kv=kv, P_sl=P_sl, npad_=npad_: e.matmul(
                                        psAT[:], V_sb[:, 512 * tkb_ + 128 * kv:512 * tkb_ + 128 * kv + 128],
                                        P_sl, start=(tkb_ == 0), stop=(tkb_ == npad_ - 1)),
                                        inc=PE)
                                    pv_last[hh0 + half] = PE.n
                                slot_cons[sl_] = PE.n

                            for tkb in range(npad):
                                sl = tkb % 2
                                for half in range(2):
                                    hh = hh0 + half
                                    h = 8 * g2 + hh
                                    kv = h // 4
                                    ph = pA if half == 0 else pB
                                    wts = []
                                    if half == 0:
                                        wts = W_((AC, prev_exp),
                                                 (DV, nDVng if (hh0 == 0 and tkb < 2) else 0),
                                                 (DV, araw_free[(hh0 - 2) % 8] if tkb == 0 else 0),
                                                 (DV, araw_free[(hh0 - 1) % 8] if tkb == 0 else 0))
                                    w.emit("tensor", lambda e, kv=kv, tkb=tkb, hh=hh, ph=ph: e.matmul(
                                        ph[:], K_sb[:, 4096 * kv + 128 * tkb:4096 * kv + 128 * (tkb + 1)],
                                        qT[:, 512 * hh:512 * (hh + 1)], start=True, stop=True),
                                        wts, inc=PE if half == 1 else None)
                                nPEs = PE.n
                                if pend is not None:
                                    emit_lpv(*pend)
                                    pend = None
                                pb_sl = Pbuf[:, 1024 * sl:1024 * sl + 1024]
                                nACe = w.emit("scalar", lambda e, pb_sl=pb_sl, p2i=p2i: e.activation(
                                    pb_sl, pS2[:], ACTF.Exp, bias=negM2[:, p2i:p2i + 1],
                                    scale=SCALE),
                                    W_((PE, nPEs), (PE, slot_cons[sl])), inc=AC)
                                prev_exp = nACe
                                masked = tkb >= npad - 8
                                if masked:
                                    j = tkb - (npad - 8)
                                    nDVm2 = 0
                                    for half in range(2):
                                        hsl = Pbuf[:, 1024 * sl + 512 * half:
                                                   1024 * sl + 512 * half + 512]
                                        nDVm2 = w.emit("vector", lambda e, hsl=hsl, j=j: e.tensor_tensor(
                                            hsl, hsl, mask_sb[:, 512 * j:512 * j + 512],
                                            AL.mult), W_((AC, nACe), (SM, nldm)))
                                    pend = ([(DV, nDVm2)], sl, tkb)
                                else:
                                    pend = ([(AC, nACe)], sl, tkb)
                                drip(1)
                            emit_lpv(*pend)
                            lstop_pe = PE.n
                            for half in range(2):
                                hh = hh0 + half
                                h = 8 * g2 + hh
                                psAT = pC if half == 0 else pD
                                araw_free[hh] = w.emit("vector", lambda e, h=h, psAT=psAT: e.tensor_copy(
                                    gattn[:, 512 * h:512 * (h + 1)], psAT[:]),
                                    W_((PE, pv_last[hh]), (DV, araw_gmult[hh])), inc=DV)
                            drip(2)
                        bank_exp["last"] = prev_exp
                        nDVr = w.emit("vector", lambda e: e.reciprocal(
                            gil8[:], pE[0:8, :]), W_((PE, lstop_pe), (DV, nDVgates)),
                            inc=DV)
                        drn(w)
                        nDVg = w.emit("vector", lambda e, g2=g2: e.tensor_tensor(
                            gil8[:], g8[:, 512 * g2:512 * g2 + 512], gil8[:], AL.mult),
                            inc=DV)
                        qn_prev_dv = nDVr
                        ghprev = 0
                        for hh in range(8):
                            h = 8 * g2 + hh
                            nPEgr = w.emit("tensor", lambda e, hh=hh: e.matmul(
                                pH[:], rsel_sb[:, 128 * hh:128 * (hh + 1)], gil8[:],
                                start=True, stop=True),
                                W_((DV, nDVg), (DV, ghprev)), inc=PE)
                            ghprev = w.emit("vector", lambda e, h=h: e.tensor_tensor(
                                gattn[:, 512 * h:512 * (h + 1)],
                                gattn[:, 512 * h:512 * (h + 1)], pH[:], AL.mult),
                                [(PE, nPEgr), (DV, araw_free[hh])], inc=DV)
                            araw_gmult[hh] = ghprev
                    nDVgat = DV.n
                    mask_cons = DV.n
                    hT_free_pe = PE.n
                    assert gen["g"] is None, f"slot {s}: deferred LN2 not drained"
                    tr_done = {}
                    xnt_dv = {}
                    for c in range(NCH):
                        r = c % 2
                        nGPo = dma(w, "gpsimd",
                                   pan_sb[:, 2048 * r:2048 * r + 2048],
                                   wo_p[c], GPAN[r],
                                   W_((DV, panel_cons.get(r, 0)),
                                      (AC, tr_done.get(c - 2, 0))))
                        nldx = dma(w, "sync", xtr[:],
                                   xT_own[128 * c:128 * (c + 1), 512 * s:512 * (s + 1)],
                                   SXTR, W_((DV, xnt_dv.get(c - 1, 0))))
                        psO = pA if r == 0 else pB
                        for hd in range(16):
                            w.emit("tensor", lambda e, hd=hd, r=r, psO=psO: e.matmul(
                                psO[:], pan_sb[:, 2048 * r + 128 * hd:2048 * r + 128 * hd + 128],
                                gattn[:, 512 * hd:512 * (hd + 1)],
                                start=(hd == 0), stop=(hd == 15)),
                                W_((GPAN[r], nGPo), (DV, nDVgat),
                                   (AC, bank_exp["last"] if c < 2 else 0))
                                if hd == 0 else [], inc=PE if hd == 15 else None)
                        nPEo = PE.n
                        panel_cons[r] = 0
                        nDVx = w.emit("vector", lambda e, r=r, c=c, psO=psO: e.scalar_tensor_tensor(
                            xnt[:, 512 * r:512 * r + 512], psO[:], ga_t[:, c:c + 1],
                            xtr[:], AL.mult, AL.add),
                            W_((PE, nPEo), (SXTR, nldx), (STXN, STXN.n)),
                            inc=DV)
                        xnt_dv[c] = nDVx
                        panel_cons[r] = nDVx
                        xnt_store[c] = store(w, xnT_h[c][:, 512 * s:512 * (s + 1)],
                                             xnt[:, 512 * r:512 * r + 512],
                                             STXN, [(DV, nDVx)])
                        ptx = pH if c % 2 == 0 else pC
                        for u in range(4):
                            w.emit("tensor", lambda e, r=r, u=u, ptx=ptx: e.transpose(
                                ptx[:, 128 * u:128 * (u + 1)],
                                xnt[:, 512 * r + 128 * u:512 * r + 128 * (u + 1)],
                                idf[:]),
                                W_((DV, nDVx), (AC, tr_done.get(c - 2, 0)))
                                if u == 0 else [], inc=PE if u == 3 else None)
                        nPEt = PE.n
                        dst4 = xntok[:].rearrange("p (u n) -> p u n", n=2048)[:, :, 128 * c:128 * (c + 1)]
                        tr_done[c] = w.emit("scalar", lambda e, dst4=dst4, ptx=ptx: e.copy(
                            dst4, ptx[:].rearrange("p (u n) -> p u n", n=128)),
                            [(PE, nPEt)], inc=AC)
                    gen["g"] = ln2_gen(s, tr_done[15])
                while gen["g"] is not None:
                    drip(1)
                w.fence(ALLS)
                with nc.Block() as blk:
                    _replay(blk, w)

        with contextlib.ExitStack() as p3s:
            sb3 = lambda n, shp, d: p3s.enter_context(nc.sbuf_tensor(n, shp, d))
            wgu_ld = sb3("wgu_ld", [128, NCH * 1024], BF16)
            trn_sb = sb3("trn_sb", [128, 2 * NCH * 1024], BF16)
            Xb = sb3("Xb", [128, 1024], BF16)
            h2q_sb = sb3("h2q_sb", [128, 16 * C], BF16)
            scg_all = sb3("scg_all", [128, 16], F32)
            asil = sb3("asil", [128, 512], F32)
            msb = sb3("msb", [128, 2 * 512], F32)
            wdb = sb3("wdb", [128, FF], BF16)
            rowf = sb3("rowf3", [1, 640], F32)

            w = Walk(DV)
            w.fence(ALLS)
            nPEps = w.emit("tensor", lambda e: e.matmul(
                pE[0:1, 0:2], onesf_sb[:], psc_acc[:], start=True, stop=True), inc=PE)
            w.emit("vector", lambda e: e.tensor_copy(rowf[0:1, 0:2], pE[0:1, 0:2]),
                   [(PE, nPEps)])
            drn(w)
            w.emit("vector", lambda e: e.tensor_scalar(
                rowf[0:1, 2:3], rowf[0:1, 0:1], 1.0 / NGU, float(EPS), AL.mult, AL.max))
            drn(w)
            w.emit("vector", lambda e: e.tensor_scalar(
                rowf[0:1, 3:4], rowf[0:1, 1:2], 1.0 / ND, float(EPS), AL.mult, AL.max))
            drn(w)
            for col, src_c, f in [(8, 2, 0.5), (9, 2, -0.5), (11, 3, 0.5), (12, 3, -0.5)]:
                w.emit("vector", lambda e, col=col, src_c=src_c, f=f: e.tensor_scalar(
                    rowf[0:1, col:col + 1], rowf[0:1, src_c:src_c + 1], f, None, AL.mult))
                drn(w)
            w.emit("vector", lambda e: e.tensor_copy(rowf[0:1, 10:11], rowf[0:1, 2:3]))
            drn(w)
            nDVth = w.emit("vector", lambda e: e.tensor_copy(
                rowf[0:1, 13:14], rowf[0:1, 3:4]), inc=DV)
            nPErep = w.emit("tensor", lambda e: e.matmul(
                pH[:, 0:6], row1f_sb[:], rowf[0:1, 8:14], start=True, stop=True),
                [(DV, nDVth)], inc=PE)
            nDVgt = w.emit("vector", lambda e: e.tensor_copy(
                gthr[:, 0:6], pH[:, 0:6]), [(PE, nPErep)], inc=DV)
            drn(w)
            nDVsc = w.emit("vector", lambda e: e.tensor_scalar(
                scg_all[:], invs2_all[:], gthr[:, 2:3], -1.0, AL.mult, AL.mult),
                [(DV, nDVgt)], inc=DV)
            h2ld = {}
            for t in range(16):
                h2ld[t] = dma(w, "sync", h2q_sb[:, 2048 * t:2048 * (t + 1)], h2q_h[t], SH2)
            wd = {"i": 0, "st": {}}

            def wd_step(w):
                ci = wd["i"]
                if ci >= NCH:
                    return
                wd["i"] += 1
                bsl = wdb[:]
                nGPd = dma(w, "gpsimd", bsl, wd_p[ci],
                           GWD, W_((STTRN, wd["st"].get(ci - 1, 0))))
                for hhf in range(8):
                    hsl = bsl[:, 1024 * hhf:1024 * hhf + 1024]
                    w.emit("vector", lambda e, hsl=hsl: e.tensor_scalar(
                        Xb[:], hsl, gthr[:, 4:5], -1.0, AL.is_gt, AL.add),
                        W_((GWD, nGPd), (DV, nDVgt),
                           (DV, DV.n if hhf == 0 else 0)))
                    drn(w)
                    w.emit("vector", lambda e, hsl=hsl: e.scalar_tensor_tensor(
                        hsl, hsl, gthr[:, 3:4], Xb[:], AL.is_ge, AL.add))
                    drn(w)
                nDVt = w.emit("vector", lambda e: e.tensor_copy(
                    rowf[0:1, 16:17], rowf[0:1, 13:14]), inc=DV)
                wd["st"][ci] = store(w, trn_h[ci], bsl[:], STTRN, [(DV, nDVt)])

            gu_dv = {}
            gu_pe = {}
            mid_store = {}
            ngpg = {}

            def tern_chunk(jj, q):
                r = jj % 2
                tsl = trn_sb[:, 16384 * r:16384 * r + 16384]
                src = wgu_ld[:, 1024 * q:1024 * (q + 1)]
                dst = tsl[:, 1024 * q:1024 * (q + 1)]
                w.emit("vector", lambda e, src=src: e.tensor_scalar(
                    Xb[:], src, gthr[:, 1:2], -1.0, AL.is_gt, AL.add),
                    W_((GGU, ngpg[jj]), (DV, nDVgt), (PE, gu_pe.get(jj - 2, 0)),
                       (DV, DV.n))
                    if q == 0 else W_((GGU, ngpg[jj])))
                drn(w)
                if q == 15:
                    gu_dv[jj] = w.emit("vector", lambda e, src=src, dst=dst: e.scalar_tensor_tensor(
                        dst, src, gthr[:, 0:1], Xb[:], AL.is_ge, AL.add), inc=DV)
                else:
                    w.emit("vector", lambda e, src=src, dst=dst: e.scalar_tensor_tensor(
                        dst, src, gthr[:, 0:1], Xb[:], AL.is_ge, AL.add))
                    drn(w)

            ngpg[0] = dma(w, "gpsimd", wgu_ld[:], wgu_pr[0], GGU, [(DV, nDVgt)])
            for q in range(16):
                tern_chunk(0, q)
            for j in range(16):
                r = j % 2
                tsl = trn_sb[:, 16384 * r:16384 * r + 16384]
                pend_tern = list(range(16)) if j + 1 < 16 else []
                mid_dv = {}
                mid_ac = {}
                for t in range(16):
                    if t == 0 and j + 1 < 16:
                        ngpg[j + 1] = dma(w, "gpsimd", wgu_ld[:], wgu_pr[j + 1],
                                          GGU, W_((DV, gu_dv[j])))
                    pa2 = pA if t % 2 == 0 else pC
                    pb2 = pB if t % 2 == 0 else pD
                    for c in range(NCH):
                        wts = W_((DV, gu_dv[j]), (SH2, SH2.n),
                                 (DV, mid_dv.get(t - 2, 0)),
                                 (AC, mid_ac.get(t - 2, 0))) if c == 0 else []
                        w.emit("tensor", lambda e, c=c, t=t, pa2=pa2, tsl=tsl: e.matmul(
                            pa2[:], h2q_sb[:, 2048 * t + 128 * c:2048 * t + 128 * c + 128],
                            tsl[:, 1024 * c:1024 * c + 512],
                            start=(c == 0), stop=(c == 15)), wts)
                        w.emit("tensor", lambda e, c=c, t=t, pb2=pb2, tsl=tsl: e.matmul(
                            pb2[:], h2q_sb[:, 2048 * t + 128 * c:2048 * t + 128 * c + 128],
                            tsl[:, 1024 * c + 512:1024 * (c + 1)],
                            start=(c == 0), stop=(c == 15)),
                            inc=PE if c == 15 else None)
                    nPEab = PE.n
                    mid_ac[t] = w.emit("scalar", lambda e, t=t, pa2=pa2: e.activation(
                        asil[:], pa2[:], ACTF.Silu, scale=scg_all[:, t:t + 1]),
                        W_((PE, nPEab), (DV, nDVsc), (DV, mid_dv.get(t - 1, 0))), inc=AC)
                    rm = t % 2
                    mid_dv[t] = w.emit("vector", lambda e, t=t, pb2=pb2, rm=rm: e.scalar_tensor_tensor(
                        msb[:, 512 * rm:512 * rm + 512], pb2[:], scg_all[:, t:t + 1],
                        asil[:], AL.mult, AL.mult),
                        W_((AC, mid_ac[t]), (STMD[rm], STMD[rm].n)),
                        inc=DV)
                    mid_store[(j, t)] = store(w, mid_h[t][:, 512 * j:512 * (j + 1)],
                                              msb[:, 512 * rm:512 * rm + 512],
                                              STMD[rm], [(DV, mid_dv[t])])
                    if t >= 2:
                        for _ in range(2 if t < 4 else 1):
                            if pend_tern:
                                tern_chunk(j + 1, pend_tern.pop(0))
                gu_pe[j] = PE.n
                wd_step(w)
            w.fence(ALLS)
            with nc.Block() as blk:
                _replay(blk, w)

        with contextlib.ExitStack() as p4s:
            sb4 = lambda n, shp, d: p4s.enter_context(nc.sbuf_tensor(n, shp, d))
            mid_sb = sb4("mid_sb", [128, FF], F32)
            mq_sb = sb4("mq_sb", [128, 64 * 512], BF16)
            trn4 = sb4("trn4", [128, 2 * FF], BF16)
            xr4 = sb4("xr4", [128, 2 * 512], F32)
            ob4 = sb4("ob4", [128, 2 * 512], F32)
            q4d = sb4("q4d", [128, 1024], I32)
            q4e = sb4("q4e", [128, 1024], F32)
            q4r = sb4("q4r", [128, 1024], F32)
            q4big = sb4("q4big", [128, FF], BF16)
            q4big2 = sb4("q4big2", [128, FF], BF16)
            t4 = sb4("t4", [128, 512], F32)
            smrep = sb4("smrep", [128, 512], F32)
            rowf = sb4("rowf4", [1, 640], F32)

            w = Walk(DV)
            w.fence(ALLS)
            SMD2 = sems["SMD2"]
            nld = {}
            mid_cons = {}
            inv_dv = {}
            ndvn = {}
            tq_cc = {}
            nacmq = {}
            mq_free = {"v": 0}

            def ld_mid(i):
                if i >= 16:
                    return
                sm_ = SMD if i % 2 == 0 else SMD2
                nld[i] = dma(w, "sync", mid_sb[:], mid_h[i], sm_,
                             W_((DV, mid_cons.get(i - 1, 0))))

            def q_stage(i):
                sm_ = SMD if i % 2 == 0 else SMD2
                msl = mid_sb[:]
                w.emit("vector", lambda e, msl=msl: e.tensor_reduce(
                    sml[:, 2:3], msl, AXX, AL.max, apply_absolute_value=True),
                    W_((sm_, nld[i])))
                drn(w)
                w.emit("vector", lambda e: e.tensor_scalar(
                    sml[:, 2:3], sml[:, 2:3], float(EPS), None, AL.max))
                drn(w)
                w.emit("vector", lambda e: e.tensor_scalar(
                    sml[:, 5:6], sml[:, 2:3], 1.0 / 127.0, None, AL.mult))
                drn(w)
                w.emit("vector", lambda e: e.reciprocal(sml[:, 3:4], sml[:, 2:3]))
                drn(w)
                nDVs4 = w.emit("vector", lambda e: e.tensor_scalar(
                    sml[:, 4:5], sml[:, 3:4], 127.0, None, AL.mult), inc=DV)
                nPEi = w.emit("tensor", lambda e: e.transpose(
                    pE[0:1, 0:128], sml[:, 5:6], idf[:]),
                    W_((DV, nDVs4), (DV, inv_dv.get(i - 1, 0))), inc=PE)
                inv_dv[i] = w.emit("vector", lambda e, i=i: e.tensor_copy(
                    rowf[0:1, 128 * (i % 4):128 * (i % 4) + 128], pE[0:1, 0:128]),
                    [(PE, nPEi)], inc=DV)
                yield
                qb = q4big if i % 2 == 0 else q4big2
                for cc in range(8):
                    csl = msl[:, 1024 * cc:1024 * (cc + 1)]
                    w.emit("vector", lambda e, csl=csl: e.tensor_scalar(
                        q4d[:], csl, sml[:, 4:5], None, AL.mult),
                        W_((PE, tq_cc.get((i - 2, cc), 0))))
                    drn(w)
                    w.emit("vector", lambda e, csl=csl: e.scalar_tensor_tensor(
                        q4e[:], csl, sml[:, 4:5], q4d[:], AL.mult, AL.subtract))
                    drn(w)
                    w.emit("vector", lambda e: e.scalar_tensor_tensor(
                        q4r[:], q4e[:], 0.5, q4d[:], AL.is_gt, AL.add))
                    drn(w)
                    ndvn[(i, cc)] = w.emit("vector", lambda e, cc=cc, qb=qb: e.scalar_tensor_tensor(
                        qb[:, 1024 * cc:1024 * (cc + 1)], q4e[:], -0.5, q4r[:],
                        AL.is_lt, AL.subtract), inc=DV)
                    yield
                mid_cons[i] = DV.n
                ld_mid(i + 1)
                yield

            qg = {"g": None}
            qpend = []

            def qdrip(n=1):
                for _ in range(n):
                    if qg["g"] is None:
                        if not qpend:
                            return
                        qg["g"] = q_stage(qpend.pop(0))
                    try:
                        next(qg["g"])
                    except StopIteration:
                        qg["g"] = None

            def q_drain():
                while qg["g"] is not None or qpend:
                    qdrip(1)

            def t_stage(i):
                tt = i % 4
                qb = q4big if i % 2 == 0 else q4big2
                for cc in range(8):
                    for qq in range(2):
                        pt = pT1 if qq == 0 else pT2
                        for u in range(4):
                            uu = 4 * qq + u
                            w.emit("tensor", lambda e, uu=uu, u=u, pt=pt, i=i, cc=cc, qb=qb: e.transpose(
                                pt[:, 128 * u:128 * (u + 1)],
                                qb[:, 1024 * cc + 128 * uu:1024 * cc + 128 * (uu + 1)],
                                idb[:]),
                                W_((DV, ndvn[(i, cc)]),
                                   (PE, mq_free["v"] if (i % 4 == 0 and cc == 0 and qq == 0) else 0),
                                   (AC, AC.n if qq == 0 else 0))
                                if u == 0 else [], inc=PE if u == 3 else None)
                        nPEt4 = PE.n
                        dst4 = mq_sb[:].rearrange("p (f n) -> p f n", n=512)[
                            :, 8 * cc + 4 * qq:8 * cc + 4 * qq + 4,
                            128 * tt:128 * tt + 128]
                        w.emit("scalar", lambda e, dst4=dst4, pt=pt: e.copy(
                            dst4, pt[:].rearrange("p (u n) -> p u n", n=128)),
                            [(PE, nPEt4)], inc=AC)
                    tq_cc[(i, cc)] = PE.n

            ld_mid(0)
            qpend.extend([0, 1])
            q_drain()
            for g in range(4):
                for tt in range(4):
                    i = 4 * g + tt
                    if tt == 0:
                        q_drain()
                    t_stage(i)
                    nxt = i + 2
                    if nxt <= 15 and nxt % 4 >= 2:
                        qpend.append(nxt)
                        q_drain()
                nacmq[g] = AC.n
                w.emit("vector", lambda e: e.tensor_scalar(
                    rowf[0:1, 0:512], rowf[0:1, 0:512], gthr[0:1, 5:6], -1.0,
                    AL.mult, AL.mult), W_((DV, inv_dv[4 * g + 3]), (DV, DV.n)))
                drn(w)
                nDVsm = w.emit("vector", lambda e: e.tensor_copy(
                    rowf[0:1, 520:521], rowf[0:1, 0:1]), inc=DV)
                nPEsm = w.emit("tensor", lambda e: e.matmul(
                    pH[:], row1f_sb[:], rowf[0:1, 0:512], start=True, stop=True),
                    [(DV, nDVsm)], inc=PE)
                nDVsr = w.emit("vector", lambda e: e.tensor_copy(smrep[:], pH[:]),
                               [(PE, nPEsm)], inc=DV)
                if g < 3:
                    qpend.extend([4 * (g + 1), 4 * (g + 1) + 1])
                ob_dv = {}
                ob_st = {}
                for c in range(NCH):
                    r = c % 2
                    tsl = trn4[:, FF * r:FF * r + FF]
                    nGP4 = dma(w, "gpsimd", tsl, trn_h[c], GT4[r],
                               W_((DV, ob_dv.get(c - 2, DV.n if c < 2 else 0))))
                    nldx4 = dma(w, "sync", xr4[:, 512 * r:512 * r + 512],
                                xnT_h[c][:, 512 * g:512 * (g + 1)],
                                SXR[r], W_((DV, ob_dv.get(c - 2, DV.n if c < 2 else 0))))
                    psO2 = pA if r == 0 else pB
                    for f in range(64):
                        w.emit("tensor", lambda e, f=f, psO2=psO2, tsl=tsl: e.matmul(
                            psO2[:], tsl[:, 128 * f:128 * (f + 1)],
                            mq_sb[:, 512 * f:512 * (f + 1)],
                            start=(f == 0), stop=(f == 63)),
                            W_((GT4[r], nGP4), (AC, nacmq[g]), (DV, nDVsr),
                               (DV, ob_dv.get(c - 2, 0))) if f == 0 else [],
                            inc=PE if f == 63 else None)
                    nPE4 = PE.n
                    w.emit("vector", lambda e, psO2=psO2: e.tensor_tensor(
                        t4[:], psO2[:], smrep[:], AL.mult),
                        W_((PE, nPE4), (SXR[r], nldx4)))
                    drn(w)
                    nDVo = w.emit("vector", lambda e, c=c, r=r: e.scalar_tensor_tensor(
                        ob4[:, 512 * r:512 * r + 512], t4[:], gb_t[:, c:c + 1],
                        xr4[:, 512 * r:512 * r + 512], AL.mult, AL.add),
                        W_((STOUT, STOUT.n)), inc=DV)
                    ob_dv[c] = nDVo
                    ob_st[c] = store(w, outT[128 * c:128 * (c + 1), 512 * g:512 * (g + 1)],
                                     ob4[:, 512 * r:512 * r + 512], STOUT, [(DV, nDVo)])
                    if c >= 2:
                        qdrip(2)
                mq_free["v"] = PE.n
            w.emit("sync", lambda e: e.nop(),
                   W_((STOUT, STOUT.n), (STXN, STXN.n), (STH2, STH2.n),
                      (STMD[0], STMD[0].n), (STMD[1], STMD[1].n), (STTRN, STTRN.n)))
            with nc.Block() as blk:
                _replay(blk, w)

    return nc


def _host_inputs(inputs):
    x0 = np.ascontiguousarray(np.asarray(inputs["L0_attn_hidden_states"], np.float32))
    T = lambda m: np.ascontiguousarray(np.asarray(m, np.float32).T)
    Wq, Wk, Wv, Wo = (inputs["L0_attn_Wq"], inputs["L0_attn_Wk"],
                      inputs["L0_attn_Wv"], inputs["L0_attn_Wo"])
    Wgs, Wgw = inputs["L0_attn_Wgslc"], inputs["L0_attn_Wgswa"]
    Wgu, Wd = inputs["L0_ffn_w_gate_up"], inputs["L0_ffn_w_down"]
    cc = np.asarray(inputs["L0_c"], np.float32)
    wmod = np.ascontiguousarray(np.asarray(inputs["L0_wmod"], np.float32))

    def sbord(wT):
        R, N = wT.shape
        return np.ascontiguousarray(
            wT.reshape(R // 128, 128, N).transpose(1, 0, 2).reshape(128, -1))

    wqT = T(Wq)
    wq_p = np.stack([sbord(wqT[:, 128 * h:128 * (h + 1)]) for h in range(H)])
    woT = T(Wo)
    wo_p = np.stack([sbord(woT[:, 128 * c:128 * (c + 1)]) for c in range(NCH)])
    wguT = T(Wgu)
    wgu_pr = np.stack([
        sbord(np.concatenate([wguT[:, 512 * j:512 * (j + 1)],
                              wguT[:, FF + 512 * j:FF + 512 * (j + 1)]], axis=1)
              .reshape(C, 1024)) for j in range(16)])
    wdT = T(Wd)
    wd_p = np.stack([sbord(wdT[:, 128 * c:128 * (c + 1)]) for c in range(NCH)])
    wmod_pp = np.stack([sbord(wmod[:, 256 * j:256 * (j + 1)]) for j in range(48)])

    inv_freq = (1.0 / 10000.0 ** (np.arange(0, D, 2, dtype=np.float32) / D)).astype(np.float32)
    t = np.arange(S, dtype=np.float32)
    fr = np.outer(t, inv_freq).astype(np.float32)
    emb = np.concatenate([fr, fr], -1)
    cosT = np.ascontiguousarray(np.cos(emb).astype(np.float32).T)
    sinT = np.ascontiguousarray(np.sin(emb).astype(np.float32).T)

    identb = np.eye(128, dtype=np.float32)
    rotTc = np.zeros((128, 128), np.float32)
    for d_ in range(64):
        rotTc[d_ + 64, d_] = -1.0
        rotTc[d_, d_ + 64] = 1.0
    oneh = np.zeros((128, 256), np.float32)
    for h in range(16):
        oneh[:, 16 * h + h] = 1.0
    khot = np.zeros((128, 64), np.float32)
    for g in range(4):
        khot[:, 16 * g + 4 * g:16 * g + 4 * g + 4] = 1.0
    rsel8f = np.zeros((8, 1024), np.float32)
    for m in range(8):
        rsel8f[m, 128 * m:128 * (m + 1)] = 1.0
    row16 = np.zeros((16, 128), np.float32)
    row16[0] = 1.0
    row1 = np.ones((1, 128), np.float32)

    shared = dict(
        wkT=sbord(T(Wk)), wvT=sbord(T(Wv)),
        wgT=sbord(np.ascontiguousarray(np.concatenate([T(Wgs), T(Wgw)], axis=1))),
        wq_p=wq_p, wo_p=wo_p, wmod=wmod_pp, wgu_pr=wgu_pr, wd_p=wd_p,
        cosF=cosT, sinF=sinT,
        identb=identb.astype(BF), identf=identb.astype(np.float32),
        rotTc=rotTc.astype(BF), oneh=oneh.astype(BF), khot=khot.astype(BF),
        rsel8f=rsel8f, row16=row16.astype(BF), row1b=row1.astype(BF),
        row1f=row1, i16b=np.eye(16, dtype=np.float32).astype(BF),
        onesf=np.ones((128, 1), np.float32),
    )

    per_core = []
    for core in range(8):
        cls, b = core // 4, core % 4
        qcs = QCS[cls]
        rows = np.concatenate([np.arange(512 * q, 512 * (q + 1)) for q in qcs])
        xb = x0[b]
        m = np.zeros((SLOTS, 8, 128, 512), np.float32)
        for si, q in enumerate(qcs):
            for j in range(8):
                tkb = PAD[si] - 8 + j
                tq = 512 * q + np.arange(512)[None, :]
                tk = 128 * tkb + np.arange(128)[:, None]
                m[si, j] = (tq >= tk).astype(np.float32)
        per_core.append(dict(
            x_kv=np.ascontiguousarray(xb),
            x_own=np.ascontiguousarray(xb[rows]),
            xT_own=np.ascontiguousarray(xb[rows].T),
            cosO=np.ascontiguousarray(cosT[:, rows]),
            sinO=np.ascontiguousarray(sinT[:, rows]),
            cvec=np.ascontiguousarray(cc[b:b + 1]),
            masks=np.ascontiguousarray(m.astype(BF)),
            **shared,
        ))
    return per_core


def kernel(**inputs):
    per_core = _host_inputs(inputs)
    nc = build()
    res = run_bass_kernel_spmd(nc, per_core, core_ids=list(range(8)))
    out = np.empty((B, S, C), np.float32)
    for core in range(8):
        cls, b = core // 4, core % 4
        qcs = QCS[cls]
        o = np.ascontiguousarray(np.asarray(res.results[core]["outT"]).T)
        for si, q in enumerate(qcs):
            out[b, 512 * q:512 * (q + 1)] = o[512 * si:512 * (si + 1)]
    return out


# version 13
# speedup vs baseline: 1.0247x; 1.0247x over previous
import contextlib
import numpy as np
import ml_dtypes

import concourse.bass as bass
import concourse.mybir as mybir
from concourse.bass_utils import run_bass_kernel_spmd

F32 = mybir.dt.float32
BF16 = mybir.dt.bfloat16
I32 = mybir.dt.int32
AL = mybir.AluOpType
ACTF = mybir.ActivationFunctionType
AXX = mybir.AxisListType.X
BF = ml_dtypes.bfloat16

B, S, C, H, HKV, D = 4, 4096, 2048, 16, 4, 128
FF = 8192
NCH = C // 128
SLOTS = 4
PAD = [8, 16, 24, 32]
QCS = [[0, 3, 4, 7], [1, 2, 5, 6]]
SCALE = float(np.float32(1.0) / np.float32(np.sqrt(np.float32(D))))
EPS = 1e-5
NGU = float(2 * FF * C)
ND = float(C * FF)

ENGS = ("sync", "gpsimd", "tensor", "vector", "scalar")


class Sem:
    __slots__ = ("h", "n")

    def __init__(self, h):
        self.h = h
        self.n = 0


class Walk:
    def __init__(self, dv=None):
        self.ops = {e: [] for e in ENGS}
        self.dv = dv
        self._dvb = 0

    def emit(self, eng, fn, waits=(), inc=None, k=1):
        if eng == "vector" and self.dv is not None:
            if inc is None:
                inc = self.dv
                k = 1
            if self._dvb > 0:
                waits = list(waits) + [(self.dv, self._dvb)]
                self._dvb = 0
        if inc is not None:
            inc.n += k
        self.ops[eng].append((fn, tuple(waits), inc, k))
        return inc.n if inc is not None else 0

    def dbar(self):
        if self.dv is not None:
            self._dvb = self.dv.n

    def wait(self, eng, waits):
        self.ops[eng].append((None, tuple(waits), None, 0))

    def fence(self, sems, engines=ENGS):
        waits = tuple((s, s.n) for s in sems if s.n > 0)
        for e in engines:
            self.ops[e].append((None, waits, None, 0))


def _replay(block, w):
    def mk(name):
        def body(eng):
            for fn, waits, inc, k in w.ops[name]:
                for sem, val in waits:
                    if val > 0:
                        eng.wait_ge(sem.h, val)
                if fn is None:
                    continue
                inst = fn(eng)
                if inc is not None:
                    inst.then_inc(inc.h, k)
        return body

    for name in ENGS:
        if w.ops[name]:
            getattr(block, name)(mk(name))


def build():
    nc = bass.Bass("TRN2", target_bir_lowering=False)
    dtt = lambda n, shp, d, k: nc.dram_tensor(n, shp, d, kind=k).ap()

    x_kv = dtt("x_kv", [S, C], F32, "ExternalInput")
    x_own = dtt("x_own", [2048, C], F32, "ExternalInput")
    xT_own = dtt("xT_own", [C, 2048], F32, "ExternalInput")
    cosF = dtt("cosF", [D, S], F32, "ExternalInput")
    sinF = dtt("sinF", [D, S], F32, "ExternalInput")
    cosO = dtt("cosO", [D, 2048], F32, "ExternalInput")
    sinO = dtt("sinO", [D, 2048], F32, "ExternalInput")
    cvec = dtt("cvec", [1, C], F32, "ExternalInput")
    wkT = dtt("wkT", [128, NCH * 512], F32, "ExternalInput")
    wvT = dtt("wvT", [128, NCH * 512], F32, "ExternalInput")
    wgT = dtt("wgT", [128, NCH * 32], F32, "ExternalInput")
    wq_p = dtt("wq_p", [H, 128, NCH * 128], F32, "ExternalInput")
    wo_p = dtt("wo_p", [NCH, 128, NCH * 128], F32, "ExternalInput")
    wmod = dtt("wmod", [48, 128, NCH * 256], F32, "ExternalInput")
    wgu_pr = dtt("wgu_pr", [16, 128, NCH * 1024], F32, "ExternalInput")
    wd_p = dtt("wd_p", [NCH, 128, 64 * 128], F32, "ExternalInput")
    masks = dtt("masks", [SLOTS, 8, 128, 512], BF16, "ExternalInput")
    identb = dtt("identb", [128, 128], BF16, "ExternalInput")
    identf = dtt("identf", [128, 128], F32, "ExternalInput")
    rotTc = dtt("rotTc", [128, 128], BF16, "ExternalInput")
    oneh = dtt("oneh", [128, 256], BF16, "ExternalInput")
    khot = dtt("khot", [128, 64], BF16, "ExternalInput")
    rsel8f = dtt("rsel8f", [8, 1024], F32, "ExternalInput")
    row16 = dtt("row16", [16, 128], BF16, "ExternalInput")
    row1b = dtt("row1b", [1, 128], BF16, "ExternalInput")
    row1f = dtt("row1f", [1, 128], F32, "ExternalInput")
    i16b = dtt("i16b", [16, 16], BF16, "ExternalInput")
    onesf = dtt("onesf", [128, 1], F32, "ExternalInput")

    mid_h = dtt("mid_h", [16, 128, FF], F32, "Internal")
    xnT_h = dtt("xnT_h", [NCH, 128, 2048], F32, "Internal")
    h2q_h = dtt("h2q_h", [16, 128, C], BF16, "Internal")
    trn_h = dtt("trn_h", [NCH, 128, FF], BF16, "Internal")
    outT = dtt("outT", [C, 2048], F32, "ExternalOutput")

    es = contextlib.ExitStack()
    sbuf = lambda n, shp, d: es.enter_context(nc.sbuf_tensor(n, shp, d))
    psum = lambda n, shp, d: es.enter_context(nc.psum_tensor(n, shp, d))

    with es:
        idb = sbuf("idb", [128, 128], BF16)
        idf = sbuf("idf", [128, 128], F32)
        rot_sb = sbuf("rot_sb", [128, 128], BF16)
        oneh_sb = sbuf("oneh_sb", [128, 256], BF16)
        khot_sb = sbuf("khot_sb", [128, 64], BF16)
        rsel_sb = sbuf("rsel_sb", [8, 1024], F32)
        row16_sb = sbuf("row16_sb", [16, 128], BF16)
        row1b_sb = sbuf("row1b_sb", [1, 128], BF16)
        row1f_sb = sbuf("row1f_sb", [1, 128], F32)
        i16_sb = sbuf("i16_sb", [16, 16], BF16)
        onesf_sb = sbuf("onesf_sb", [128, 1], F32)
        sa_t = sbuf("sa_t", [128, 16], F32)
        capa_t = sbuf("capa_t", [128, 16], F32)
        ga_t = sbuf("ga_t", [128, 16], F32)
        gb_t = sbuf("gb_t", [128, 16], F32)
        capb_rep = sbuf("capb_rep", [128, C], BF16)
        sbb_rep = sbuf("sbb_rep", [128, C], BF16)
        kmax2 = sbuf("kmax2", [16, 1], F32)
        invs2_all = sbuf("invs2_all", [128, 16], F32)
        gthr = sbuf("gthr", [128, 6], F32)
        psc_acc = sbuf("psc_acc", [128, 2], F32)
        bnst = sbuf("bnst", [128, 24], F32)
        bnag = sbuf("bnag", [128, 2], F32)
        istd = sbuf("istd", [128, 1], F32)
        bias2 = sbuf("bias2", [128, 1], F32)
        sml = sbuf("sml", [128, 6], F32)
        s16c = sbuf("s16c", [16, 6], F32)
        g8 = sbuf("g8", [8, 1024], BF16)
        gil8 = sbuf("gil8", [8, 512], F32)
        d16 = sbuf("d16", [16, 16], BF16)
        negM = sbuf("negM", [128, 16], F32)
        silT = sbuf("silT", [128, 16], BF16)
        epsc = sbuf("epsc", [128, 1], F32)

        pS2 = psum("pS2", [128, 1024], F32)
        pA = pS2[:, 0:512]
        pB = pS2[:, 512:1024]
        pC = psum("pC", [128, 512], F32)
        pD = psum("pD", [128, 512], F32)
        pE = psum("pE", [16, 512], F32)
        pH = psum("pH", [128, 512], F32)
        pT1 = psum("pT1", [128, 512], BF16)
        pT2 = psum("pT2", [128, 512], BF16)

        semnames = ["PE", "DV", "AC",
                    "CONST", "SX0", "SX1", "SC0", "SC1", "SM", "SXTR", "SMD", "SMD2",
                    "SH2", "SXR0", "SXR1",
                    "GW", "GM0", "GM1", "GM2", "GM3", "GPAN0", "GPAN1", "GPS0", "GPS1",
                    "GPS2", "GPS3",
                    "GGU", "GWD", "GT40", "GT41",
                    "STXN", "STH2", "STH2B", "STMD0", "STMD1", "STTRN", "STOUT"]
        sems = {k: Sem(es.enter_context(nc.semaphore(k))) for k in semnames}
        PE, DV, AC = (sems[k] for k in ["PE", "DV", "AC"])
        SX = [sems["SX0"], sems["SX1"]]
        SC = [sems["SC0"], sems["SC1"]]
        GM = [sems["GM0"], sems["GM1"]]
        GPAN = [sems["GPAN0"], sems["GPAN1"]]
        GPS = [sems["GPS0"], sems["GPS1"]]
        GT4 = [sems["GT40"], sems["GT41"]]
        SXR = [sems["SXR0"], sems["SXR1"]]
        STMD = [sems["STMD0"], sems["STMD1"]]
        CONST, SM, SXTR, SMD, SH2 = (sems[k] for k in ["CONST", "SM", "SXTR", "SMD", "SH2"])
        GW, GGU, GWD = (sems[k] for k in ["GW", "GGU", "GWD"])
        STXN, STH2, STTRN, STOUT = (sems[k] for k in ["STXN", "STH2", "STTRN", "STOUT"])
        ALLS = list(sems.values())

        def dma(w, q, dst, src, sem, waits=()):
            return w.emit(q, lambda e, d=dst, s=src: e.dma_start(out=d, in_=s),
                          [x for x in waits if x[1] > 0], inc=sem, k=16)

        def store(w, dst, src, sem, waits=()):
            return w.emit("sync", lambda e, d=dst, s=src: e.dma_start(out=d, in_=s),
                          [x for x in waits if x[1] > 0], inc=sem, k=16)

        def drn(w):
            w.dbar()

        def W_(*xs):
            return [x for x in xs if x[1] > 0]

        def ln_tile(w, xt, tt, waits_x, hT_sb, xnb, ev_waits=(), norm_waits=(),
                    banks=None):
            for u in range(4):
                w.emit("vector", lambda e, u=u, xt=xt: e.bn_stats(
                    bnst[:, 6 * u:6 * u + 6], xt[:, 512 * u:512 * (u + 1)]),
                    waits_x if u == 0 else ())
            drn(w)
            w.emit("vector", lambda e: e.bn_aggr(bnag[:], bnst[:]))
            drn(w)
            nDVst = w.emit("vector", lambda e: e.tensor_copy(sml[:, 0:1], bnag[:, 1:2]),
                           inc=DV)
            nACsq = w.emit("scalar", lambda e: e.activation(
                sml[:, 1:2], sml[:, 0:1], ACTF.Sqrt, bias=epsc[:]),
                [(DV, nDVst)], inc=AC)
            w.emit("vector", lambda e: e.reciprocal(istd[:], sml[:, 1:2]), [(AC, nACsq)])
            drn(w)
            nDVb = w.emit("vector", lambda e: e.scalar_tensor_tensor(
                bias2[:], bnag[:, 0:1], -1.0, istd[:], AL.mult, AL.mult), inc=DV)
            nACn = w.emit("scalar", lambda e, xt=xt, xnb=xnb: e.activation(
                xnb[:], xt[:], ACTF.Identity, bias=bias2[:], scale=istd[:]),
                [(DV, nDVb)] + list(norm_waits), inc=AC)
            if banks is None:
                banks = (pT1, pT2)
            nb = len(banks)
            evg = {}
            for q in range(4):
                pt = banks[q % nb]
                for u in range(4):
                    c = 4 * q + u
                    w.emit("tensor", lambda e, c=c, u=u, pt=pt, xnb=xnb: e.transpose(
                        pt[:, 128 * u:128 * (u + 1)], xnb[:, 128 * c:128 * (c + 1)],
                        idb[:]),
                        (W_((AC, nACn), (DV, DV.n)) if q == 0
                         else W_((DV, evg.get(q - nb, 0)))) if u == 0 else [],
                        inc=PE if u == 3 else None)
                nPEt = PE.n
                for u in range(4):
                    c = 4 * q + u
                    evg[q] = w.emit("vector", lambda e, c=c, u=u, pt=pt, tt=tt, hT_sb=hT_sb: e.tensor_scalar(
                        hT_sb[:, 512 * c + 128 * tt:512 * c + 128 * tt + 128],
                        pt[:, 128 * u:128 * (u + 1)],
                        capa_t[:, c:c + 1], sa_t[:, c:c + 1], AL.mult, AL.add),
                        ([(PE, nPEt)] + list(ev_waits if q == 0 else ()))
                        if u == 0 else [], inc=DV)
            return PE.n, DV.n

        def rope_emit(w, dst_slice, cos_ap, sin_ap, kpre, t1s, t2s, nPEq,
                      pk=None, pr=None, pr_free=0):
            if pk is None:
                pk, pr = pA, pB
            nACk = w.emit("scalar", lambda e, k=kpre, pk=pk: e.copy(k[:], pk[:]),
                          [(PE, nPEq)], inc=AC)
            nPEr = w.emit("tensor", lambda e, k=kpre, pr=pr: e.matmul(
                pr[:], rot_sb[:], k[:], start=True, stop=True),
                W_((AC, nACk), (DV, pr_free)), inc=PE)
            w.emit("vector", lambda e, t1=t1s, ca=cos_ap, pk=pk: e.tensor_tensor(
                t1[:], pk[:], ca, AL.mult), [(AC, nACk)])
            dv_pk = DV.n
            drn(w)
            w.emit("vector", lambda e, t2=t2s, sa=sin_ap, pr=pr: e.tensor_tensor(
                t2[:], pr[:], sa, AL.mult), [(PE, nPEr)])
            dv_pr = DV.n
            drn(w)
            nDVk = w.emit("vector", lambda e, d=dst_slice, t1=t1s, t2=t2s:
                          e.tensor_tensor(d, t1[:], t2[:], AL.add), inc=DV)
            return nDVk, nACk, dv_pk, dv_pr

        with contextlib.ExitStack() as p0s:
            sb0 = lambda n, shp, d: p0s.enter_context(nc.sbuf_tensor(n, shp, d))
            cv_sb = sb0("cv_sb", [1, C], F32)
            silr = sb0("silr", [1, C], F32)
            silb = sb0("silb", [1, C], BF16)
            modrow = sb0("modrow", [1, 6 * C], F32)
            modbf = sb0("modbf", [1, 2 * C], BF16)
            wm_sb = sb0("wm_sb", [128, 4 * 4096], BF16)
            p0scr = sb0("p0scr", [128, 48], F32)
            p0msk = sb0("p0msk", [128, 16], I32)

            w = Walk(DV)
            consts = [(identb, idb), (identf, idf), (rotTc, rot_sb), (oneh, oneh_sb),
                      (khot, khot_sb), (rsel8f, rsel_sb), (row16, row16_sb),
                      (row1b, row1b_sb), (row1f, row1f_sb), (i16b, i16_sb),
                      (onesf, onesf_sb), (cvec, cv_sb)]
            for src, dst in consts:
                dma(w, "sync", dst[:], src[:], CONST)
            nld0 = CONST.n
            w.emit("vector", lambda e: e.memset(kmax2[:], 0.0))
            drn(w)
            w.emit("vector", lambda e: e.memset(epsc[:], float(EPS)))
            drn(w)
            w.emit("vector", lambda e: e.memset(psc_acc[:], 0.0), inc=DV)
            nACs = w.emit("scalar", lambda e: e.activation(silr[:], cv_sb[:], ACTF.Silu),
                          [(CONST, nld0)], inc=AC)
            nDVs = w.emit("vector", lambda e: e.tensor_copy(silb[:], silr[:]),
                          [(AC, nACs)], inc=DV)
            prevsil = 0
            for j in range(16):
                nPEsil = w.emit("tensor", lambda e, j=j: e.transpose(
                    pT1[:, 0:1], silb[0:1, 128 * j:128 * (j + 1)], idb[0:1, 0:1]),
                    W_((DV, nDVs if j == 0 else 0), (DV, prevsil)), inc=PE)
                prevsil = w.emit("vector", lambda e, j=j: e.tensor_copy(
                    silT[:, j:j + 1], pT1[:, 0:1]), [(PE, nPEsil)], inc=DV)
            nDVst = prevsil
            modpe = {}
            modev = {}
            for j in range(48):
                r = j % 2
                r4 = j % 4
                sem4 = [GM[0], GM[1], sems["GM2"], sems["GM3"]][r4]
                bsl = wm_sb[:, 4096 * r4:4096 * r4 + 4096]
                ngp = dma(w, "gpsimd", bsl, wmod[j],
                          sem4, W_((PE, modpe.get(j - 4, 0))))
                pj = pE if r == 0 else pH
                for c in range(NCH):
                    w.emit("tensor", lambda e, c=c, pj=pj, r4=r4: e.matmul(
                        pj[0:1, 0:256], silT[:, c:c + 1],
                        wm_sb[:, 4096 * r4 + 256 * c:
                              4096 * r4 + 256 * (c + 1)][:, :],
                        start=(c == 0), stop=(c == 15)),
                        W_((sem4, ngp), (DV, nDVst), (DV, modev.get(j - 2, 0)))
                        if c == 0 else [], inc=PE if c == 15 else None)
                modpe[j] = PE.n
                modev[j] = w.emit("vector", lambda e, j=j, pj=pj: e.tensor_copy(
                    modrow[0:1, 256 * j:256 * (j + 1)], pj[0:1, 0:256]),
                    [(PE, modpe[j])], inc=DV)
            drn(w)
            w.emit("vector", lambda e: e.tensor_scalar(
                modbf[0:1, 0:C], modrow[0:1, 4 * C:5 * C], 1.0, None, AL.add),
                [(DV, modev[47])])
            drn(w)
            nDVm = w.emit("vector", lambda e: e.tensor_copy(
                modbf[0:1, C:2 * C], modrow[0:1, 3 * C:4 * C]), inc=DV)
            drn(w)
            prevev = nDVm
            for v in range(2):
                for kk in range(4):
                    nPEr = w.emit("tensor", lambda e, v=v, kk=kk: e.matmul(
                        pH[:], row1b_sb[:],
                        modbf[0:1, C * v + 512 * kk:C * v + 512 * kk + 512],
                        start=True, stop=True), [(DV, prevev)], inc=PE)
                    prevev = w.emit("vector", lambda e, v=v, kk=kk: e.tensor_copy(
                        (capb_rep if v == 0 else sbb_rep)[:, 512 * kk:512 * kk + 512],
                        pH[:]), [(PE, nPEr)], inc=DV)
            for name, vi in [("sa", 0), ("ca", 1), ("ga", 2), ("gb", 5)]:
                for j in range(16):
                    w.emit("tensor", lambda e, vi=vi, j=j: e.transpose(
                        pH[:, j:j + 1],
                        modrow[0:1, C * vi + 128 * j:C * vi + 128 * (j + 1)],
                        idf[0:1, 0:1]), W_((DV, prevev)) if j == 0 else [], inc=PE)
                nPEv = PE.n
                dst = {"sa": sa_t, "ca": capa_t, "ga": ga_t, "gb": gb_t}[name]
                prevev = w.emit("vector", lambda e, dst=dst: e.tensor_copy(
                    dst[:], pH[:, 0:16]), [(PE, nPEv)], inc=DV)
                drn(w)
            w.emit("vector", lambda e: e.tensor_scalar(
                capa_t[:], capa_t[:], 1.0, None, AL.add))
            drn(w)
            for gt in (ga_t, gb_t):
                w.emit("vector", lambda e, gt=gt: e.tensor_scalar(
                    p0msk[:], gt[:], 0.0, None, AL.is_ge))
                drn(w)
                w.emit("vector", lambda e, gt=gt: e.tensor_scalar(
                    p0scr[:, 16:32], gt[:], 0.25, None, AL.max))
                drn(w)
                w.emit("vector", lambda e, gt=gt: e.tensor_scalar(
                    p0scr[:, 32:48], gt[:], -0.25, None, AL.min))
                drn(w)
                w.emit("vector", lambda e, gt=gt: e.select(
                    gt[:], p0msk[:], p0scr[:, 16:32], p0scr[:, 32:48],
                    add_drain=True))
                drn(w)
            w.emit("vector", lambda e: e.tensor_copy(sml[:, 5:6], onesf_sb[:]), inc=DV)
            w.fence(ALLS)
            with nc.Block() as blk:
                _replay(blk, w)

        with contextlib.ExitStack() as p12:
            sb12 = lambda n, shp, d: p12.enter_context(nc.sbuf_tensor(n, shp, d))
            K_sb = sb12("K_sb", [128, HKV * S], BF16)
            V_sb = sb12("V_sb", [128, 32 * 512], BF16)
            hT = sb12("hT", [128, NCH * 512], BF16)
            xnb = sb12("xnb", [128, C], BF16)
            t1s = sb12("t1s", [128, 512], F32)
            t2s = sb12("t2s", [128, 512], F32)
            kpre = sb12("kpre", [128, 512], BF16)
            sqb = sb12("sqb", [128, 512], BF16)
            wg_sb = sb12("wg_sb", [128, NCH * 32], BF16)

            NCHK_GU, NCHK_D = 128, 64

            with contextlib.ExitStack() as p1s:
                sb1 = lambda n, shp, d: p1s.enter_context(nc.sbuf_tensor(n, shp, d))
                wk_sb = sb1("wk_sb", [128, NCH * 512], BF16)
                wv_sb = sb1("wv_sb", [128, NCH * 512], BF16)
                x_t = sb1("x_t", [128, 2 * C], F32)
                cosS = sb1("cosS", [128, 2 * 512], F32)
                sinS = sb1("sinS", [128, 2 * 512], F32)
                psc_buf = sb1("psc_buf", [128, 2 * 2048], BF16)
                psc_f32 = sb1("psc_f32", [128, 2 * 2048], F32)
                psc_dum = sb1("psc_dum", [128, 2048], BF16)
                psc_col8 = sb1("psc_col8", [128, 8], F32)
                bnstB = sb1("bnstB", [128, 24], F32)
                bnagB = sb1("bnagB", [128, 2], F32)
                istdB = sb1("istdB", [128, 1], F32)
                bias2B = sb1("bias2B", [128, 1], F32)
                smlB = sb1("smlB", [128, 6], F32)

                psc = {"i": 0, "acts": {}, "adds": {}}
                PSCQ = [("gpsimd", GPS[0]), ("sync", sems["GPS2"]),
                        ("gpsimd", GPS[1]), ("sync", sems["GPS3"])]

                def psc_step(w):
                    i = psc["i"]
                    if i >= NCHK_GU + NCHK_D:
                        return
                    psc["i"] += 1
                    h = i % 4
                    qname, qsem = PSCQ[h]
                    buf = psc_buf if h % 2 == 0 else psc_f32
                    bsl = buf[:, 2048 * (h // 2):2048 * (h // 2 + 1)]
                    if i < NCHK_GU:
                        src = wgu_pr.rearrange("m p n -> (m p n)")[
                            262144 * i:262144 * (i + 1)].rearrange("(p n) -> p n", p=128)
                        gcol = 0
                    else:
                        q = i - NCHK_GU
                        src = wd_p.rearrange("m p n -> (m p n)")[
                            262144 * q:262144 * (q + 1)].rearrange("(p n) -> p n", p=128)
                        gcol = 1
                    ng = dma(w, qname, bsl, src, qsem,
                             W_((AC, psc["acts"].get(i - 4, 0))))
                    cl = psc_col8[:, i % 8:i % 8 + 1]
                    nac = w.emit("scalar", lambda e, bsl=bsl, cl=cl: e.activation(
                        psc_dum[:], bsl[:], ACTF.Abs, accum_out=cl),
                        W_((qsem, ng), (DV, psc["adds"].get(i - 8, 0)),
                           (AC, psc["acts"].get(i - 1, 0))), inc=AC)
                    psc["acts"][i] = nac
                    psc["adds"][i] = w.emit("vector", lambda e, gcol=gcol, cl=cl: e.tensor_tensor(
                        psc_acc[:, gcol:gcol + 1], psc_acc[:, gcol:gcol + 1],
                        cl, AL.add),
                        W_((AC, nac), (DV, psc["adds"].get(i - 1, 0))), inc=DV)
                w = Walk(DV)
                w.fence(ALLS)
                dma(w, "gpsimd", wk_sb[:], wkT[:], GW)
                dma(w, "gpsimd", wv_sb[:], wvT[:], GW)
                nGPw = dma(w, "gpsimd", wg_sb[:], wgT[:], GW)
                xcons = {}
                knorm_dv = 0
                prev_pe_fin = 0
                kpkf = {}
                kprf = {}

                def ln_stats(par, xt, waits_x):
                    bn, ag, sm = ((bnst, bnag, sml) if par == 0
                                  else (bnstB, bnagB, smlB))
                    isd, b2 = (istd, bias2) if par == 0 else (istdB, bias2B)
                    for u in range(4):
                        w.emit("vector", lambda e, u=u, xt=xt, bn=bn: e.bn_stats(
                            bn[:, 6 * u:6 * u + 6], xt[:, 512 * u:512 * (u + 1)]),
                            waits_x if u == 0 else ())
                    drn(w)
                    w.emit("vector", lambda e, ag=ag, bn=bn: e.bn_aggr(ag[:], bn[:]))
                    drn(w)
                    nDVst = w.emit("vector", lambda e, sm=sm, ag=ag: e.tensor_copy(
                        sm[:, 0:1], ag[:, 1:2]), inc=DV)
                    nACsq = w.emit("scalar", lambda e, sm=sm: e.activation(
                        sm[:, 1:2], sm[:, 0:1], ACTF.Sqrt, bias=epsc[:]),
                        [(DV, nDVst)], inc=AC)
                    w.emit("vector", lambda e, isd=isd, sm=sm: e.reciprocal(
                        isd[:], sm[:, 1:2]), [(AC, nACsq)])
                    drn(w)
                    nDVb = w.emit("vector", lambda e, b2=b2, ag=ag, isd=isd: e.scalar_tensor_tensor(
                        b2[:], ag[:, 0:1], -1.0, isd[:], AL.mult, AL.mult), inc=DV)
                    return nDVb

                def ln_fin(par, xt, tt, nDVb, prev_pe):
                    isd, b2 = (istd, bias2) if par == 0 else (istdB, bias2B)
                    nACn = w.emit("scalar", lambda e, xt=xt, isd=isd, b2=b2: e.activation(
                        xnb[:], xt[:], ACTF.Identity, bias=b2[:], scale=isd[:]),
                        W_((DV, nDVb), (PE, prev_pe)), inc=AC)
                    evg = {}
                    for q in range(4):
                        pt = pT1 if q % 2 == 0 else pT2
                        for u in range(4):
                            c = 4 * q + u
                            w.emit("tensor", lambda e, c=c, u=u, pt=pt: e.transpose(
                                pt[:, 128 * u:128 * (u + 1)],
                                xnb[:, 128 * c:128 * (c + 1)], idb[:]),
                                (W_((AC, nACn), (DV, DV.n)) if q == 0
                                 else W_((DV, evg.get(q - 2, 0)))) if u == 0 else [],
                                inc=PE if u == 3 else None)
                        nPEt = PE.n
                        for u in range(4):
                            c = 4 * q + u
                            evg[q] = w.emit("vector", lambda e, c=c, u=u, pt=pt, tt=tt: e.tensor_scalar(
                                hT[:, 512 * c + 128 * tt:512 * c + 128 * tt + 128],
                                pt[:, 128 * u:128 * (u + 1)],
                                capa_t[:, c:c + 1], sa_t[:, c:c + 1], AL.mult, AL.add),
                                [(PE, nPEt)] if u == 0 else [], inc=DV)
                    return PE.n

                for tc in range(8):
                    dma(w, "sync", cosS[:, 512 * (tc % 2):512 * (tc % 2) + 512],
                        cosF[:, 512 * tc:512 * (tc + 1)], SC[tc % 2],
                        W_((DV, xcons.get(("cs", tc - 2), 0))))
                    nldc = dma(w, "sync", sinS[:, 512 * (tc % 2):512 * (tc % 2) + 512],
                               sinF[:, 512 * tc:512 * (tc + 1)], SC[tc % 2])
                    finp = None
                    for tt in range(4):
                        i = 4 * tc + tt
                        r = i % 2
                        nld = dma(w, "sync", x_t[:, C * r:C * r + C],
                                  x_kv[128 * i:128 * (i + 1), :], SX[r],
                                  W_((DV, xcons.get(i - 2, 0)),
                                     (AC, xcons.get((i - 2, "a"), 0))))
                        xt = x_t[:, C * r:C * r + C]
                        nDVb = ln_stats(r, xt, [(SX[r], nld)])
                        xcons[i] = DV.n
                        psc_step(w)
                        psc_step(w)
                        psc_step(w)
                        if finp is not None:
                            prev_pe_fin = ln_fin(finp[0], finp[1], finp[2],
                                                 finp[3], prev_pe_fin)
                            xcons[(finp[4], "a")] = AC.n
                        psc_step(w)
                        psc_step(w)
                        psc_step(w)
                        finp = (r, xt, tt, nDVb, i)
                    prev_pe_fin = ln_fin(finp[0], finp[1], finp[2],
                                         finp[3], prev_pe_fin)
                    xcons[(finp[4], "a")] = AC.n
                    hT_ready = DV.n
                    for kv in range(4):
                        par = kv % 2
                        pk = pA if par == 0 else pC
                        pr = pB if par == 0 else pD
                        pkf = kpkf.get(par, (0, 0))
                        for c in range(NCH):
                            w.emit("tensor", lambda e, kv=kv, c=c, pk=pk: e.matmul(
                                pk[:], wk_sb[:, 512 * c + 128 * kv:512 * c + 128 * kv + 128],
                                hT[:, 512 * c:512 * (c + 1)],
                                start=(c == 0), stop=(c == 15)),
                                W_((GW, nGPw), (DV, hT_ready if kv == 0 else 0),
                                   (DV, pkf[0]), (AC, AC.n)) if c == 0 else [],
                                inc=PE if c == 15 else None)
                        nPEq = PE.n
                        ksl = K_sb[:, 4096 * kv + 512 * tc:4096 * kv + 512 * tc + 512]
                        w.wait("vector", W_((SC[tc % 2], nldc)))
                        nDVk, nACk, dv_pk, dv_pr = rope_emit(
                            w, ksl,
                            cosS[:, 512 * (tc % 2):512 * (tc % 2) + 512],
                            sinS[:, 512 * (tc % 2):512 * (tc % 2) + 512],
                            kpre, t1s, t2s, nPEq, pk, pr, kprf.get(par, 0))
                        kpkf[par] = (dv_pk, nACk)
                        kprf[par] = dv_pr
                        xcons[("cs", tc)] = nDVk
                        nACq = w.emit("scalar", lambda e, ksl=ksl: e.activation(
                            sqb[:], ksl, ACTF.Square), [(DV, nDVk)], inc=AC)
                        w.emit("tensor", lambda e, kv=kv: e.matmul(
                            pE[:], khot_sb[:, 16 * kv:16 * kv + 16], sqb[:],
                            start=(kv == 0), stop=(kv == 3)),
                            W_((AC, nACq), (DV, knorm_dv)),
                            inc=PE if kv == 3 else None)
                    nPEn = PE.n
                    w.emit("vector", lambda e: e.tensor_reduce(
                        s16c[:, 0:1], pE[:], AXX, AL.max), [(PE, nPEn)])
                    drn(w)
                    knorm_dv = w.emit("vector", lambda e: e.tensor_tensor(
                        kmax2[:], kmax2[:], s16c[:, 0:1], AL.max), inc=DV)
                    drn(w)
                    for tt in range(4):
                        i = 4 * tc + tt
                        pv = pC if tt % 2 == 0 else pD
                        for c in range(NCH):
                            w.emit("tensor", lambda e, c=c, tt=tt, pv=pv: e.matmul(
                                pv[:], hT[:, 512 * c + 128 * tt:512 * c + 128 * tt + 128],
                                wv_sb[:, 512 * c:512 * (c + 1)],
                                start=(c == 0), stop=(c == 15)),
                                W_((AC, AC.n), (DV, DV.n if tt < 2 else 0))
                                if c == 0 else [],
                                inc=PE if c == 15 else None)
                        nPEv = PE.n
                        w.emit("scalar", lambda e, i=i, pv=pv: e.copy(
                            V_sb[:, 512 * i:512 * (i + 1)], pv[:]),
                            [(PE, nPEv)], inc=AC)
                assert psc["i"] == NCHK_GU + NCHK_D, psc["i"]
                w.fence(ALLS)
                with nc.Block() as blk:
                    _replay(blk, w)

            with contextlib.ExitStack() as p2s:
                sb2 = lambda n, shp, d: p2s.enter_context(nc.sbuf_tensor(n, shp, d))
                qT = sb2("qT", [128, 8 * 512], BF16)
                gattn = sb2("gattn", [128, 16 * 512], BF16)
                xo_t = sb2("xo_t", [128, C], F32)
                Pbuf = sb2("Pbuf", [128, 2 * 1024], BF16)
                negM2 = sb2("negM2", [128, 8], F32)
                mask_sb = sb2("mask_sb", [128, 8 * 512], BF16)
                pan_sb = sb2("pan_sb", [128, 2 * 2048], BF16)
                xnt = sb2("xnt", [128, 2 * 512], F32)
                xtr = sb2("xtr", [128, 512], F32)
                cosO2 = sb2("cosO2", [128, 512], F32)
                sinO2 = sb2("sinO2", [128, 512], F32)
                xntok = sb2("xntok", [128, 4 * 2048], BF16)
                h2st = sb2("h2st", [128, 2 * 512], BF16)
                q_s1 = sb2("q_s1", [128, C], F32)
                q2scr = sb2("q2scr", [128, 1024], F32)
                q_ng = sb2("q_ng", [128, 512], BF16)
                q_d = sb2("q_d", [128, 512], I32)
                ln2st = sb2("ln2st", [128, 24], F32)
                ln2ag = sb2("ln2ag", [128, 2], F32)
                ln2istd = sb2("ln2istd", [128, 1], F32)
                ln2b2 = sb2("ln2b2", [128, 1], F32)
                ln2sml = sb2("ln2sml", [128, 6], F32)

                w = Walk(DV)
                w.fence(ALLS)
                araw_free = {hh: 0 for hh in range(8)}
                araw_gmult = {hh: 0 for hh in range(8)}
                panel_cons = {}
                mask_cons = 0
                hT_free_pe = 0
                xnt_store = {}
                bank_exp = {"last": 0}

                h2ev_last = {"v": 0}
                h2st_ring = {0: 0, 1: 0}

                def ln2_gen(s, ready_ac):
                    for tt in range(4):
                        i = 4 * s + tt
                        xsl = xntok[:, 2048 * tt:2048 * (tt + 1)]
                        for u in range(4):
                            w.emit("vector", lambda e, u=u, xsl=xsl: e.bn_stats(
                                ln2st[:, 6 * u:6 * u + 6],
                                xsl[:, 512 * u:512 * (u + 1)]),
                                W_((AC, ready_ac)) if (u == 0 and tt == 0) else ())
                        drn(w)
                        w.emit("vector", lambda e: e.bn_aggr(ln2ag[:], ln2st[:]))
                        drn(w)
                        nDVst = w.emit("vector", lambda e: e.tensor_copy(
                            ln2sml[:, 0:1], ln2ag[:, 1:2]), inc=DV)
                        nACsq = w.emit("scalar", lambda e: e.activation(
                            ln2sml[:, 1:2], ln2sml[:, 0:1], ACTF.Sqrt, bias=epsc[:]),
                            [(DV, nDVst)], inc=AC)
                        yield
                        w.emit("vector", lambda e: e.reciprocal(
                            ln2istd[:], ln2sml[:, 1:2]), [(AC, nACsq)])
                        drn(w)
                        nDVb = w.emit("vector", lambda e: e.scalar_tensor_tensor(
                            ln2b2[:], ln2ag[:, 0:1], -1.0, ln2istd[:],
                            AL.mult, AL.mult), inc=DV)
                        nACn = w.emit("scalar", lambda e, xsl=xsl: e.activation(
                            q_s1[:], xsl, ACTF.Identity, bias=ln2b2[:],
                            scale=ln2istd[:]), [(DV, nDVb)], inc=AC)
                        yield
                        for ch in range(4):
                            w.emit("vector", lambda e, ch=ch: e.tensor_tensor(
                                q2scr[:, 0:512], q_s1[:, 512 * ch:512 * (ch + 1)],
                                capb_rep[:, 512 * ch:512 * (ch + 1)], AL.mult),
                                [(AC, nACn)])
                            drn(w)
                            w.emit("vector", lambda e, ch=ch: e.tensor_tensor(
                                q_s1[:, 512 * ch:512 * (ch + 1)], q2scr[:, 0:512],
                                sbb_rep[:, 512 * ch:512 * (ch + 1)], AL.add))
                            drn(w)
                            if ch == 1:
                                bar1 = DV.n
                                yield
                                w.wait("vector", [(DV, bar1)])
                        w.emit("vector", lambda e: e.tensor_reduce(
                            ln2sml[:, 2:3], q_s1[:], AXX, AL.max,
                            apply_absolute_value=True))
                        drn(w)
                        w.emit("vector", lambda e: e.tensor_scalar(
                            ln2sml[:, 2:3], ln2sml[:, 2:3], float(EPS), None, AL.max))
                        drn(w)
                        w.emit("vector", lambda e, i=i: e.tensor_scalar(
                            invs2_all[:, i:i + 1], ln2sml[:, 2:3], 1.0 / 127.0,
                            None, AL.mult))
                        drn(w)
                        w.emit("vector", lambda e: e.reciprocal(
                            ln2sml[:, 3:4], ln2sml[:, 2:3]))
                        drn(w)
                        nDVs127 = w.emit("vector", lambda e: e.tensor_scalar(
                            ln2sml[:, 4:5], ln2sml[:, 3:4], 127.0, None, AL.mult),
                            inc=DV)
                        yield
                        tp_pe = 0
                        for qc_ in range(4):
                            hsl = q_s1[:, 512 * qc_:512 * (qc_ + 1)]
                            esl = q2scr[:, 0:512]
                            rsl = q2scr[:, 512:1024]
                            w.emit("vector", lambda e, hsl=hsl: e.tensor_scalar(
                                q_d[:], hsl, ln2sml[:, 4:5], None, AL.mult),
                                W_((DV, nDVs127) if qc_ == 0 else (PE, tp_pe)))
                            drn(w)
                            w.emit("vector", lambda e, hsl=hsl, esl=esl: e.scalar_tensor_tensor(
                                esl, hsl, ln2sml[:, 4:5], q_d[:],
                                AL.mult, AL.subtract))
                            drn(w)
                            w.emit("vector", lambda e, esl=esl, rsl=rsl: e.scalar_tensor_tensor(
                                rsl, esl, 0.5, q_d[:], AL.is_gt, AL.add))
                            drn(w)
                            nq = w.emit("vector", lambda e, esl=esl, rsl=rsl: e.scalar_tensor_tensor(
                                q_ng[:], esl, -0.5, rsl, AL.is_lt, AL.subtract),
                                W_((PE, tp_pe)), inc=DV)
                            yield
                            for u in range(4):
                                w.emit("tensor", lambda e, u=u: e.transpose(
                                    pT2[:, 128 * u:128 * (u + 1)],
                                    q_ng[:, 128 * u:128 * (u + 1)], idb[:]),
                                    W_((DV, nq), (AC, h2ev_last["v"]))
                                    if u == 0 else [], inc=PE if u == 3 else None)
                            tp_pe = PE.n
                            hf = qc_ % 2
                            sh2 = STH2 if hf == 0 else sems["STH2B"]
                            h2ev_last["v"] = w.emit("scalar", lambda e, hf=hf: e.copy(
                                h2st[:, 512 * hf:512 * (hf + 1)], pT2[:]),
                                [(PE, tp_pe), (sh2, h2st_ring[hf])], inc=AC)
                            h2st_ring[hf] = store(
                                w, h2q_h[i][:, 512 * qc_:512 * (qc_ + 1)],
                                h2st[:, 512 * hf:512 * (hf + 1)], sh2,
                                [(AC, h2ev_last["v"])])
                            yield

                gen = {"g": None}

                def drip(n=1):
                    g = gen["g"]
                    if g is None:
                        return
                    for _ in range(n):
                        try:
                            next(g)
                        except StopIteration:
                            gen["g"] = None
                            return

                for s in range(SLOTS):
                    nldc2a = dma(w, "sync", cosO2[:], cosO[:, 512 * s:512 * (s + 1)],
                                 SC[0], W_((DV, DV.n)))
                    nldc2 = dma(w, "sync", sinO2[:], sinO[:, 512 * s:512 * (s + 1)],
                                SC[0])
                    nldm = 0
                    for j in range(8):
                        nldm = dma(w, "sync", mask_sb[:, 512 * j:512 * j + 512],
                                   masks[s, j], SM,
                                   W_((DV, mask_cons)) if j == 0 else ())
                    cons_q = {}
                    for tt in range(4):
                        i = 4 * s + tt
                        nld = dma(w, "sync", xo_t[:], x_own[128 * i:128 * (i + 1), :],
                                  SX[0],
                                  W_((DV, cons_q.get(tt - 1, DV.n if tt == 0 else 0)),
                                     (AC, cons_q.get((tt - 1, "a"), 0))))
                        ln_tile(w, xo_t, tt, [(SX[0], nld)], hT, xnb,
                                ev_waits=W_((PE, hT_free_pe if tt == 0 else 0)),
                                banks=(pT1,))
                        cons_q[tt] = DV.n
                        cons_q[(tt, "a")] = AC.n
                        drip(3)
                    nDVh = DV.n
                    gparts = [(0, 0), (16, 1), (8, 2), (24, 3)]
                    gprev = 0
                    for off, part in gparts:
                        for c in range(NCH):
                            w.emit("tensor", lambda e, c=c, off=off: e.matmul(
                                pE[0:8, :], wg_sb[:, 32 * c + off:32 * c + off + 8],
                                hT[:, 512 * c:512 * (c + 1)],
                                start=(c == 0), stop=(c == 15)),
                                W_((DV, nDVh), (AC, gprev)) if c == 0 else [],
                                inc=PE if c == 15 else None)
                        nPEg = PE.n
                        gg = part // 2
                        dst = (g8[:, 512 * gg:512 * gg + 512] if part % 2 == 0
                               else gil8[:])
                        gprev = w.emit("scalar", lambda e, dst=dst: e.activation(
                            dst, pE[0:8, :], ACTF.Sigmoid),
                            W_((PE, nPEg), (DV, DV.n)), inc=AC)
                        if part % 2 == 1:
                            w.emit("vector", lambda e, gg=gg: e.tensor_tensor(
                                g8[:, 512 * gg:512 * gg + 512],
                                g8[:, 512 * gg:512 * gg + 512], gil8[:], AL.add),
                                [(AC, gprev)], inc=DV)
                    nDVgates = DV.n
                    drip(2)
                    for g2 in range(2):
                        qn_prev_dv = DV.n
                        qprep = {}
                        ropeDV = {}
                        rotPE = {}
                        sq_pend = {"v": None}

                        def emit_qn_pend(qn_prev_dv=qn_prev_dv):
                            if sq_pend["v"] is None:
                                return
                            h, nacq = sq_pend["v"]
                            sq_pend["v"] = None
                            w.emit("tensor", lambda e, h=h: e.matmul(
                                pE[:], oneh_sb[:, 16 * h:16 * h + 16], sqb[:],
                                start=(h % 8 == 0), stop=(h % 8 == 7)),
                                W_((AC, nacq),
                                   (DV, qn_prev_dv if h % 8 == 0 else 0)), inc=PE)

                        def emit_qproj(hh, g2=g2):
                            h = 8 * g2 + hh
                            r = h % 2
                            pq = pA if hh % 2 == 0 else pC
                            nGPq = dma(w, "gpsimd", pan_sb[:, 2048 * r:2048 * r + 2048],
                                       wq_p[h], GPAN[r], W_((DV, panel_cons.get(r, 0))))
                            for c in range(NCH):
                                w.emit("tensor", lambda e, c=c, r=r, pq=pq: e.matmul(
                                    pq[:], pan_sb[:, 2048 * r + 128 * c:2048 * r + 128 * c + 128],
                                    hT[:, 512 * c:512 * (c + 1)],
                                    start=(c == 0), stop=(c == 15)),
                                    W_((GPAN[r], nGPq),
                                       (DV, ropeDV.get(hh - 2, DV.n if hh < 2 else 0)),
                                       (AC, AC.n if (hh < 2 and g2 == 0) else 0),
                                       (AC, bank_exp.get("last", 0) if hh < 2 else 0))
                                    if c == 0 else [], inc=PE if c == 15 else None)
                            qprep[hh] = (PE.n, pq, pB if hh % 2 == 0 else pD)

                        def emit_qfin(hh, g2=g2):
                            h = 8 * g2 + hh
                            nPEq, pq, pr = qprep[hh]
                            qsl = qT[:, 512 * hh:512 * hh + 512]
                            nACk = w.emit("scalar", lambda e, pq=pq: e.copy(kpre[:], pq[:]),
                                          W_((PE, nPEq), (PE, rotPE.get(hh - 1, 0))), inc=AC)
                            nPEr = w.emit("tensor", lambda e, pr=pr: e.matmul(
                                pr[:], rot_sb[:], kpre[:], start=True, stop=True),
                                [(AC, nACk)], inc=PE)
                            rotPE[hh] = nPEr
                            emit_qn_pend()
                            peg = PE.n
                            w.wait("vector", W_((SC[0], nldc2)))
                            w.emit("vector", lambda e, pq=pq: e.tensor_tensor(
                                t1s[:], pq[:], cosO2[:], AL.mult), [(AC, nACk)])
                            w.dbar()
                            w.emit("vector", lambda e, pr=pr: e.tensor_tensor(
                                t2s[:], pr[:], sinO2[:], AL.mult), [(PE, nPEr)])
                            w.dbar()
                            nDVq = w.emit("vector", lambda e, qsl=qsl: e.tensor_tensor(
                                qsl, t1s[:], t2s[:], AL.add))
                            ropeDV[hh] = nDVq
                            panel_cons[h % 2] = nDVq
                            nACq = w.emit("scalar", lambda e, qsl=qsl: e.activation(
                                sqb[:], qsl, ACTF.Square),
                                W_((DV, nDVq), (PE, peg)), inc=AC)
                            sq_pend["v"] = (h, nACq)

                        emit_qproj(0)
                        for hh in range(8):
                            if hh < 7:
                                emit_qproj(hh + 1)
                            emit_qfin(hh)
                        emit_qn_pend()
                        nPEqn = PE.n
                        w.emit("vector", lambda e: e.tensor_reduce(
                            s16c[:, 0:1], pE[:], AXX, AL.max), [(PE, nPEqn)])
                        drn(w)
                        w.emit("vector", lambda e: e.tensor_tensor(
                            s16c[:, 1:2], s16c[:, 0:1], kmax2[:], AL.mult))
                        drn(w)
                        nDVn = w.emit("vector", lambda e: e.tensor_copy(
                            s16c[:, 2:3], s16c[:, 1:2]), inc=DV)
                        nACn2 = w.emit("scalar", lambda e: e.activation(
                            s16c[:, 3:4], s16c[:, 2:3], ACTF.Sqrt), [(DV, nDVn)], inc=AC)
                        w.emit("vector", lambda e: e.tensor_scalar(
                            s16c[:, 4:5], s16c[:, 3:4], -SCALE, None, AL.mult),
                            [(AC, nACn2)])
                        drn(w)
                        nDVd = w.emit("vector", lambda e: e.tensor_tensor(
                            d16[:], i16_sb[:], s16c[:, 4:5].to_broadcast([16, 16]),
                            AL.mult), inc=DV)
                        nPEr2 = w.emit("tensor", lambda e: e.matmul(
                            pH[:, 0:16], row16_sb[:], d16[:], start=True, stop=True),
                            [(DV, nDVd)], inc=PE)
                        w.emit("vector", lambda e: e.tensor_copy(
                            negM[:], pH[:, 0:16]), [(PE, nPEr2)])
                        drn(w)
                        nDVng = w.emit("vector", lambda e, g2=g2: e.tensor_tensor(
                            negM2[:, 4 * g2:4 * g2 + 4],
                            negM[:, 8 * g2:8 * g2 + 8:2],
                            negM[:, 8 * g2 + 1:8 * g2 + 8:2], AL.min), inc=DV)
                        drn(w)
                        npad = PAD[s]
                        pv_last = {}
                        slot_cons = {0: 0, 1: 0}
                        prev_exp = 0
                        prev_mask = {}
                        for hh0 in range(0, 8, 2):
                            p2i = (8 * g2 + hh0) // 2
                            pend = None

                            def emit_lpv(pw, sl_, tkb_, g2=g2, hh0=hh0):
                                npad_ = PAD[s]
                                for half in range(2):
                                    hh = hh0 + half
                                    h = 8 * g2 + hh
                                    kv = h // 4
                                    P_sl = Pbuf[:, 1024 * sl_ + 512 * half:
                                                1024 * sl_ + 512 * half + 512]
                                    lh = oneh_sb[:, 16 * h + (8 if h >= 8 else 0):
                                                 16 * h + (8 if h >= 8 else 0) + 8]
                                    w.emit("tensor", lambda e, lh=lh, P_sl=P_sl, hh0=hh0, tkb_=tkb_, half=half, npad_=npad_: e.matmul(
                                        pE[0:8, :], lh, P_sl,
                                        start=(hh0 == 0 and tkb_ == 0 and half == 0),
                                        stop=(hh0 == 6 and tkb_ == npad_ - 1 and half == 1)),
                                        W_(*pw,
                                           (DV, qn_prev_dv if (hh0 == 0 and tkb_ == 0 and half == 0) else 0)))
                                    psAT = pC if half == 0 else pD
                                    w.emit("tensor", lambda e, psAT=psAT, tkb_=tkb_, kv=kv, P_sl=P_sl, npad_=npad_: e.matmul(
                                        psAT[:], V_sb[:, 512 * tkb_ + 128 * kv:512 * tkb_ + 128 * kv + 128],
                                        P_sl, start=(tkb_ == 0), stop=(tkb_ == npad_ - 1)),
                                        inc=PE)
                                    pv_last[hh0 + half] = PE.n
                                slot_cons[sl_] = PE.n

                            for tkb in range(npad):
                                sl = tkb % 2
                                for half in range(2):
                                    hh = hh0 + half
                                    h = 8 * g2 + hh
                                    kv = h // 4
                                    ph = pA if half == 0 else pB
                                    wts = []
                                    if half == 0:
                                        wts = W_((AC, prev_exp),
                                                 (DV, nDVng if (hh0 == 0 and tkb < 2) else 0),
                                                 (DV, araw_free[(hh0 - 2) % 8] if tkb == 0 else 0),
                                                 (DV, araw_free[(hh0 - 1) % 8] if tkb == 0 else 0))
                                    w.emit("tensor", lambda e, kv=kv, tkb=tkb, hh=hh, ph=ph: e.matmul(
                                        ph[:], K_sb[:, 4096 * kv + 128 * tkb:4096 * kv + 128 * (tkb + 1)],
                                        qT[:, 512 * hh:512 * (hh + 1)], start=True, stop=True),
                                        wts, inc=PE if half == 1 else None)
                                nPEs = PE.n
                                if pend is not None:
                                    emit_lpv(*pend)
                                    pend = None
                                pb_sl = Pbuf[:, 1024 * sl:1024 * sl + 1024]
                                nACe = w.emit("scalar", lambda e, pb_sl=pb_sl, p2i=p2i: e.activation(
                                    pb_sl, pS2[:], ACTF.Exp, bias=negM2[:, p2i:p2i + 1],
                                    scale=SCALE),
                                    W_((PE, nPEs), (PE, slot_cons[sl])), inc=AC)
                                prev_exp = nACe
                                masked = tkb >= npad - 8
                                if masked:
                                    j = tkb - (npad - 8)
                                    nDVm2 = 0
                                    for half in range(2):
                                        hsl = Pbuf[:, 1024 * sl + 512 * half:
                                                   1024 * sl + 512 * half + 512]
                                        nDVm2 = w.emit("vector", lambda e, hsl=hsl, j=j: e.tensor_tensor(
                                            hsl, hsl, mask_sb[:, 512 * j:512 * j + 512],
                                            AL.mult), W_((AC, nACe), (SM, nldm)))
                                    pend = ([(DV, nDVm2)], sl, tkb)
                                else:
                                    pend = ([(AC, nACe)], sl, tkb)
                                drip(1)
                            emit_lpv(*pend)
                            lstop_pe = PE.n
                            for half in range(2):
                                hh = hh0 + half
                                h = 8 * g2 + hh
                                psAT = pC if half == 0 else pD
                                araw_free[hh] = w.emit("vector", lambda e, h=h, psAT=psAT: e.tensor_copy(
                                    gattn[:, 512 * h:512 * (h + 1)], psAT[:]),
                                    W_((PE, pv_last[hh]), (DV, araw_gmult[hh])), inc=DV)
                            drip(2)
                        bank_exp["last"] = prev_exp
                        nDVr = w.emit("vector", lambda e: e.reciprocal(
                            gil8[:], pE[0:8, :]), W_((PE, lstop_pe), (DV, nDVgates)),
                            inc=DV)
                        drn(w)
                        nDVg = w.emit("vector", lambda e, g2=g2: e.tensor_tensor(
                            gil8[:], g8[:, 512 * g2:512 * g2 + 512], gil8[:], AL.mult),
                            inc=DV)
                        qn_prev_dv = nDVr
                        ghprev = 0
                        for hh in range(8):
                            h = 8 * g2 + hh
                            nPEgr = w.emit("tensor", lambda e, hh=hh: e.matmul(
                                pH[:], rsel_sb[:, 128 * hh:128 * (hh + 1)], gil8[:],
                                start=True, stop=True),
                                W_((DV, nDVg), (DV, ghprev)), inc=PE)
                            ghprev = w.emit("vector", lambda e, h=h: e.tensor_tensor(
                                gattn[:, 512 * h:512 * (h + 1)],
                                gattn[:, 512 * h:512 * (h + 1)], pH[:], AL.mult),
                                [(PE, nPEgr), (DV, araw_free[hh])], inc=DV)
                            araw_gmult[hh] = ghprev
                    nDVgat = DV.n
                    mask_cons = DV.n
                    hT_free_pe = PE.n
                    assert gen["g"] is None, f"slot {s}: deferred LN2 not drained"
                    tr_done = {}
                    xnt_dv = {}
                    for c in range(NCH):
                        r = c % 2
                        nGPo = dma(w, "gpsimd",
                                   pan_sb[:, 2048 * r:2048 * r + 2048],
                                   wo_p[c], GPAN[r],
                                   W_((DV, panel_cons.get(r, 0)),
                                      (AC, tr_done.get(c - 2, 0))))
                        nldx = dma(w, "sync", xtr[:],
                                   xT_own[128 * c:128 * (c + 1), 512 * s:512 * (s + 1)],
                                   SXTR, W_((DV, xnt_dv.get(c - 1, 0))))
                        psO = pA if r == 0 else pB
                        for hd in range(16):
                            w.emit("tensor", lambda e, hd=hd, r=r, psO=psO: e.matmul(
                                psO[:], pan_sb[:, 2048 * r + 128 * hd:2048 * r + 128 * hd + 128],
                                gattn[:, 512 * hd:512 * (hd + 1)],
                                start=(hd == 0), stop=(hd == 15)),
                                W_((GPAN[r], nGPo), (DV, nDVgat),
                                   (AC, bank_exp["last"] if c < 2 else 0))
                                if hd == 0 else [], inc=PE if hd == 15 else None)
                        nPEo = PE.n
                        panel_cons[r] = 0
                        nDVx = w.emit("vector", lambda e, r=r, c=c, psO=psO: e.scalar_tensor_tensor(
                            xnt[:, 512 * r:512 * r + 512], psO[:], ga_t[:, c:c + 1],
                            xtr[:], AL.mult, AL.add),
                            W_((PE, nPEo), (SXTR, nldx), (STXN, STXN.n)),
                            inc=DV)
                        xnt_dv[c] = nDVx
                        panel_cons[r] = nDVx
                        xnt_store[c] = store(w, xnT_h[c][:, 512 * s:512 * (s + 1)],
                                             xnt[:, 512 * r:512 * r + 512],
                                             STXN, [(DV, nDVx)])
                        ptx = pH if c % 2 == 0 else pC
                        for u in range(4):
                            w.emit("tensor", lambda e, r=r, u=u, ptx=ptx: e.transpose(
                                ptx[:, 128 * u:128 * (u + 1)],
                                xnt[:, 512 * r + 128 * u:512 * r + 128 * (u + 1)],
                                idf[:]),
                                W_((DV, nDVx), (AC, tr_done.get(c - 2, 0)))
                                if u == 0 else [], inc=PE if u == 3 else None)
                        nPEt = PE.n
                        dst4 = xntok[:].rearrange("p (u n) -> p u n", n=2048)[:, :, 128 * c:128 * (c + 1)]
                        tr_done[c] = w.emit("scalar", lambda e, dst4=dst4, ptx=ptx: e.copy(
                            dst4, ptx[:].rearrange("p (u n) -> p u n", n=128)),
                            [(PE, nPEt)], inc=AC)
                    gen["g"] = ln2_gen(s, tr_done[15])
                while gen["g"] is not None:
                    drip(1)
                w.fence(ALLS)
                with nc.Block() as blk:
                    _replay(blk, w)

        with contextlib.ExitStack() as p3s:
            sb3 = lambda n, shp, d: p3s.enter_context(nc.sbuf_tensor(n, shp, d))
            wgu_ld = sb3("wgu_ld", [128, NCH * 1024], BF16)
            trn_sb = sb3("trn_sb", [128, 2 * NCH * 1024], BF16)
            Xb = sb3("Xb", [128, 1024], BF16)
            h2q_sb = sb3("h2q_sb", [128, 16 * C], BF16)
            scg_all = sb3("scg_all", [128, 16], F32)
            asil = sb3("asil", [128, 512], F32)
            msb = sb3("msb", [128, 2 * 512], F32)
            wdb = sb3("wdb", [128, FF], BF16)
            rowf = sb3("rowf3", [1, 640], F32)

            w = Walk(DV)
            w.fence(ALLS)
            nPEps = w.emit("tensor", lambda e: e.matmul(
                pE[0:1, 0:2], onesf_sb[:], psc_acc[:], start=True, stop=True), inc=PE)
            w.emit("vector", lambda e: e.tensor_copy(rowf[0:1, 0:2], pE[0:1, 0:2]),
                   [(PE, nPEps)])
            drn(w)
            w.emit("vector", lambda e: e.tensor_scalar(
                rowf[0:1, 2:3], rowf[0:1, 0:1], 1.0 / NGU, float(EPS), AL.mult, AL.max))
            drn(w)
            w.emit("vector", lambda e: e.tensor_scalar(
                rowf[0:1, 3:4], rowf[0:1, 1:2], 1.0 / ND, float(EPS), AL.mult, AL.max))
            drn(w)
            for col, src_c, f in [(8, 2, 0.5), (9, 2, -0.5), (11, 3, 0.5), (12, 3, -0.5)]:
                w.emit("vector", lambda e, col=col, src_c=src_c, f=f: e.tensor_scalar(
                    rowf[0:1, col:col + 1], rowf[0:1, src_c:src_c + 1], f, None, AL.mult))
                drn(w)
            w.emit("vector", lambda e: e.tensor_copy(rowf[0:1, 10:11], rowf[0:1, 2:3]))
            drn(w)
            nDVth = w.emit("vector", lambda e: e.tensor_copy(
                rowf[0:1, 13:14], rowf[0:1, 3:4]), inc=DV)
            nPErep = w.emit("tensor", lambda e: e.matmul(
                pH[:, 0:6], row1f_sb[:], rowf[0:1, 8:14], start=True, stop=True),
                [(DV, nDVth)], inc=PE)
            nDVgt = w.emit("vector", lambda e: e.tensor_copy(
                gthr[:, 0:6], pH[:, 0:6]), [(PE, nPErep)], inc=DV)
            drn(w)
            nDVsc = w.emit("vector", lambda e: e.tensor_scalar(
                scg_all[:], invs2_all[:], gthr[:, 2:3], -1.0, AL.mult, AL.mult),
                [(DV, nDVgt)], inc=DV)
            h2ld = {}
            for t in range(16):
                h2ld[t] = dma(w, "sync", h2q_sb[:, 2048 * t:2048 * (t + 1)], h2q_h[t], SH2)
            wd = {"i": 0, "st": {}}

            def wd_step(w):
                ci = wd["i"]
                if ci >= NCH:
                    return
                wd["i"] += 1
                bsl = wdb[:]
                nGPd = dma(w, "gpsimd", bsl, wd_p[ci],
                           GWD, W_((STTRN, wd["st"].get(ci - 1, 0))))
                for hhf in range(8):
                    hsl = bsl[:, 1024 * hhf:1024 * hhf + 1024]
                    w.emit("vector", lambda e, hsl=hsl: e.tensor_scalar(
                        Xb[:], hsl, gthr[:, 4:5], -1.0, AL.is_gt, AL.add),
                        W_((GWD, nGPd), (DV, nDVgt),
                           (DV, DV.n if hhf == 0 else 0)))
                    drn(w)
                    w.emit("vector", lambda e, hsl=hsl: e.scalar_tensor_tensor(
                        hsl, hsl, gthr[:, 3:4], Xb[:], AL.is_ge, AL.add))
                    drn(w)
                nDVt = w.emit("vector", lambda e: e.tensor_copy(
                    rowf[0:1, 16:17], rowf[0:1, 13:14]), inc=DV)
                wd["st"][ci] = store(w, trn_h[ci], bsl[:], STTRN, [(DV, nDVt)])

            gu_dv = {}
            gu_pe = {}
            mid_store = {}
            ngpg = {}

            def tern_chunk(jj, q):
                r = jj % 2
                tsl = trn_sb[:, 16384 * r:16384 * r + 16384]
                src = wgu_ld[:, 1024 * q:1024 * (q + 1)]
                dst = tsl[:, 1024 * q:1024 * (q + 1)]
                w.emit("vector", lambda e, src=src: e.tensor_scalar(
                    Xb[:], src, gthr[:, 1:2], -1.0, AL.is_gt, AL.add),
                    W_((GGU, ngpg[jj]), (DV, nDVgt), (PE, gu_pe.get(jj - 2, 0)),
                       (DV, DV.n))
                    if q == 0 else W_((GGU, ngpg[jj])))
                drn(w)
                if q == 15:
                    gu_dv[jj] = w.emit("vector", lambda e, src=src, dst=dst: e.scalar_tensor_tensor(
                        dst, src, gthr[:, 0:1], Xb[:], AL.is_ge, AL.add), inc=DV)
                else:
                    w.emit("vector", lambda e, src=src, dst=dst: e.scalar_tensor_tensor(
                        dst, src, gthr[:, 0:1], Xb[:], AL.is_ge, AL.add))
                    drn(w)

            ngpg[0] = dma(w, "gpsimd", wgu_ld[:], wgu_pr[0], GGU, [(DV, nDVgt)])
            for q in range(16):
                tern_chunk(0, q)
            for j in range(16):
                r = j % 2
                tsl = trn_sb[:, 16384 * r:16384 * r + 16384]
                pend_tern = list(range(16)) if j + 1 < 16 else []
                mid_dv = {}
                mid_ac = {}
                for t in range(16):
                    if t == 0 and j + 1 < 16:
                        ngpg[j + 1] = dma(w, "gpsimd", wgu_ld[:], wgu_pr[j + 1],
                                          GGU, W_((DV, gu_dv[j])))
                    pa2 = pA if t % 2 == 0 else pC
                    pb2 = pB if t % 2 == 0 else pD
                    for c in range(NCH):
                        wts = W_((DV, gu_dv[j]), (SH2, SH2.n),
                                 (DV, mid_dv.get(t - 2, 0)),
                                 (AC, mid_ac.get(t - 2, 0))) if c == 0 else []
                        w.emit("tensor", lambda e, c=c, t=t, pa2=pa2, tsl=tsl: e.matmul(
                            pa2[:], h2q_sb[:, 2048 * t + 128 * c:2048 * t + 128 * c + 128],
                            tsl[:, 1024 * c:1024 * c + 512],
                            start=(c == 0), stop=(c == 15)), wts)
                        w.emit("tensor", lambda e, c=c, t=t, pb2=pb2, tsl=tsl: e.matmul(
                            pb2[:], h2q_sb[:, 2048 * t + 128 * c:2048 * t + 128 * c + 128],
                            tsl[:, 1024 * c + 512:1024 * (c + 1)],
                            start=(c == 0), stop=(c == 15)),
                            inc=PE if c == 15 else None)
                    nPEab = PE.n
                    mid_ac[t] = w.emit("scalar", lambda e, t=t, pa2=pa2: e.activation(
                        asil[:], pa2[:], ACTF.Silu, scale=scg_all[:, t:t + 1]),
                        W_((PE, nPEab), (DV, nDVsc), (DV, mid_dv.get(t - 1, 0))), inc=AC)
                    rm = t % 2
                    mid_dv[t] = w.emit("vector", lambda e, t=t, pb2=pb2, rm=rm: e.scalar_tensor_tensor(
                        msb[:, 512 * rm:512 * rm + 512], pb2[:], scg_all[:, t:t + 1],
                        asil[:], AL.mult, AL.mult),
                        W_((AC, mid_ac[t]), (STMD[rm], STMD[rm].n)),
                        inc=DV)
                    mid_store[(j, t)] = store(w, mid_h[t][:, 512 * j:512 * (j + 1)],
                                              msb[:, 512 * rm:512 * rm + 512],
                                              STMD[rm], [(DV, mid_dv[t])])
                    if t >= 2:
                        for _ in range(2 if t < 4 else 1):
                            if pend_tern:
                                tern_chunk(j + 1, pend_tern.pop(0))
                gu_pe[j] = PE.n
                wd_step(w)
            w.fence(ALLS)
            with nc.Block() as blk:
                _replay(blk, w)

        with contextlib.ExitStack() as p4s:
            sb4 = lambda n, shp, d: p4s.enter_context(nc.sbuf_tensor(n, shp, d))
            mid_sb = sb4("mid_sb", [128, FF], F32)
            mq_sb = sb4("mq_sb", [128, 64 * 512], BF16)
            trn4 = sb4("trn4", [128, 2 * FF], BF16)
            xr4 = sb4("xr4", [128, 2 * 512], F32)
            ob4 = sb4("ob4", [128, 2 * 512], F32)
            q4d = sb4("q4d", [128, 1024], I32)
            q4e = sb4("q4e", [128, 1024], F32)
            q4r = sb4("q4r", [128, 1024], F32)
            q4big = sb4("q4big", [128, FF], BF16)
            q4big2 = sb4("q4big2", [128, FF], BF16)
            t4 = sb4("t4", [128, 512], F32)
            smrep = sb4("smrep", [128, 512], F32)
            rowf = sb4("rowf4", [1, 640], F32)

            w = Walk(DV)
            w.fence(ALLS)
            SMD2 = sems["SMD2"]
            nld = {}
            mid_cons = {}
            inv_dv = {}
            ndvn = {}
            tq_cc = {}
            nacmq = {}
            mq_free = {"v": 0}

            def ld_mid(i):
                if i >= 16:
                    return
                sm_ = SMD if i % 2 == 0 else SMD2
                nld[i] = dma(w, "sync", mid_sb[:], mid_h[i], sm_,
                             W_((DV, mid_cons.get(i - 1, 0))))

            def q_stage(i):
                sm_ = SMD if i % 2 == 0 else SMD2
                msl = mid_sb[:]
                w.emit("vector", lambda e, msl=msl: e.tensor_reduce(
                    sml[:, 2:3], msl, AXX, AL.max, apply_absolute_value=True),
                    W_((sm_, nld[i])))
                drn(w)
                w.emit("vector", lambda e: e.tensor_scalar(
                    sml[:, 2:3], sml[:, 2:3], float(EPS), None, AL.max))
                drn(w)
                w.emit("vector", lambda e: e.tensor_scalar(
                    sml[:, 5:6], sml[:, 2:3], 1.0 / 127.0, None, AL.mult))
                drn(w)
                w.emit("vector", lambda e: e.reciprocal(sml[:, 3:4], sml[:, 2:3]))
                drn(w)
                nDVs4 = w.emit("vector", lambda e: e.tensor_scalar(
                    sml[:, 4:5], sml[:, 3:4], 127.0, None, AL.mult), inc=DV)
                nPEi = w.emit("tensor", lambda e: e.transpose(
                    pE[0:1, 0:128], sml[:, 5:6], idf[:]),
                    W_((DV, nDVs4), (DV, inv_dv.get(i - 1, 0))), inc=PE)
                inv_dv[i] = w.emit("vector", lambda e, i=i: e.tensor_copy(
                    rowf[0:1, 128 * (i % 4):128 * (i % 4) + 128], pE[0:1, 0:128]),
                    [(PE, nPEi)], inc=DV)
                yield
                qb = q4big if i % 2 == 0 else q4big2
                for cc in range(8):
                    csl = msl[:, 1024 * cc:1024 * (cc + 1)]
                    w.emit("vector", lambda e, csl=csl: e.tensor_scalar(
                        q4d[:], csl, sml[:, 4:5], None, AL.mult),
                        W_((PE, tq_cc.get((i - 2, cc), 0))))
                    drn(w)
                    w.emit("vector", lambda e, csl=csl: e.scalar_tensor_tensor(
                        q4e[:], csl, sml[:, 4:5], q4d[:], AL.mult, AL.subtract))
                    drn(w)
                    w.emit("vector", lambda e: e.scalar_tensor_tensor(
                        q4r[:], q4e[:], 0.5, q4d[:], AL.is_gt, AL.add))
                    drn(w)
                    ndvn[(i, cc)] = w.emit("vector", lambda e, cc=cc, qb=qb: e.scalar_tensor_tensor(
                        qb[:, 1024 * cc:1024 * (cc + 1)], q4e[:], -0.5, q4r[:],
                        AL.is_lt, AL.subtract), inc=DV)
                    yield
                mid_cons[i] = DV.n
                ld_mid(i + 1)
                yield

            qg = {"g": None}
            qpend = []

            def qdrip(n=1):
                for _ in range(n):
                    if qg["g"] is None:
                        if not qpend:
                            return
                        qg["g"] = q_stage(qpend.pop(0))
                    try:
                        next(qg["g"])
                    except StopIteration:
                        qg["g"] = None

            def q_drain():
                while qg["g"] is not None or qpend:
                    qdrip(1)

            def t_stage(i):
                tt = i % 4
                qb = q4big if i % 2 == 0 else q4big2
                for cc in range(8):
                    for qq in range(2):
                        pt = pT1 if qq == 0 else pT2
                        for u in range(4):
                            uu = 4 * qq + u
                            w.emit("tensor", lambda e, uu=uu, u=u, pt=pt, i=i, cc=cc, qb=qb: e.transpose(
                                pt[:, 128 * u:128 * (u + 1)],
                                qb[:, 1024 * cc + 128 * uu:1024 * cc + 128 * (uu + 1)],
                                idb[:]),
                                W_((DV, ndvn[(i, cc)]),
                                   (PE, mq_free["v"] if (i % 4 == 0 and cc == 0 and qq == 0) else 0),
                                   (AC, AC.n if qq == 0 else 0))
                                if u == 0 else [], inc=PE if u == 3 else None)
                        nPEt4 = PE.n
                        dst4 = mq_sb[:].rearrange("p (f n) -> p f n", n=512)[
                            :, 8 * cc + 4 * qq:8 * cc + 4 * qq + 4,
                            128 * tt:128 * tt + 128]
                        w.emit("scalar", lambda e, dst4=dst4, pt=pt: e.copy(
                            dst4, pt[:].rearrange("p (u n) -> p u n", n=128)),
                            [(PE, nPEt4)], inc=AC)
                    tq_cc[(i, cc)] = PE.n

            ld_mid(0)
            qpend.extend([0, 1])
            q_drain()
            for g in range(4):
                for tt in range(4):
                    i = 4 * g + tt
                    if tt == 0:
                        q_drain()
                    t_stage(i)
                    nxt = i + 2
                    if nxt <= 15 and nxt % 4 >= 2:
                        qpend.append(nxt)
                        q_drain()
                nacmq[g] = AC.n
                w.emit("vector", lambda e: e.tensor_scalar(
                    rowf[0:1, 0:512], rowf[0:1, 0:512], gthr[0:1, 5:6], -1.0,
                    AL.mult, AL.mult), W_((DV, inv_dv[4 * g + 3]), (DV, DV.n)))
                drn(w)
                nDVsm = w.emit("vector", lambda e: e.tensor_copy(
                    rowf[0:1, 520:521], rowf[0:1, 0:1]), inc=DV)
                nPEsm = w.emit("tensor", lambda e: e.matmul(
                    pH[:], row1f_sb[:], rowf[0:1, 0:512], start=True, stop=True),
                    [(DV, nDVsm)], inc=PE)
                nDVsr = w.emit("vector", lambda e: e.tensor_copy(smrep[:], pH[:]),
                               [(PE, nPEsm)], inc=DV)
                if g < 3:
                    qpend.extend([4 * (g + 1), 4 * (g + 1) + 1])
                ob_dv = {}
                ob_st = {}
                for c in range(NCH):
                    r = c % 2
                    tsl = trn4[:, FF * r:FF * r + FF]
                    nGP4 = dma(w, "gpsimd", tsl, trn_h[c], GT4[r],
                               W_((DV, ob_dv.get(c - 2, DV.n if c < 2 else 0))))
                    nldx4 = dma(w, "sync", xr4[:, 512 * r:512 * r + 512],
                                xnT_h[c][:, 512 * g:512 * (g + 1)],
                                SXR[r], W_((DV, ob_dv.get(c - 2, DV.n if c < 2 else 0))))
                    psO2 = pA if r == 0 else pB
                    for f in range(64):
                        w.emit("tensor", lambda e, f=f, psO2=psO2, tsl=tsl: e.matmul(
                            psO2[:], tsl[:, 128 * f:128 * (f + 1)],
                            mq_sb[:, 512 * f:512 * (f + 1)],
                            start=(f == 0), stop=(f == 63)),
                            W_((GT4[r], nGP4), (AC, nacmq[g]), (DV, nDVsr),
                               (DV, ob_dv.get(c - 2, 0))) if f == 0 else [],
                            inc=PE if f == 63 else None)
                    nPE4 = PE.n
                    w.emit("vector", lambda e, psO2=psO2: e.tensor_tensor(
                        t4[:], psO2[:], smrep[:], AL.mult),
                        W_((PE, nPE4), (SXR[r], nldx4)))
                    drn(w)
                    nDVo = w.emit("vector", lambda e, c=c, r=r: e.scalar_tensor_tensor(
                        ob4[:, 512 * r:512 * r + 512], t4[:], gb_t[:, c:c + 1],
                        xr4[:, 512 * r:512 * r + 512], AL.mult, AL.add),
                        W_((STOUT, STOUT.n)), inc=DV)
                    ob_dv[c] = nDVo
                    ob_st[c] = store(w, outT[128 * c:128 * (c + 1), 512 * g:512 * (g + 1)],
                                     ob4[:, 512 * r:512 * r + 512], STOUT, [(DV, nDVo)])
                    if c >= 2:
                        qdrip(2)
                mq_free["v"] = PE.n
            w.emit("sync", lambda e: e.nop(),
                   W_((STOUT, STOUT.n), (STXN, STXN.n), (STH2, STH2.n),
                      (STMD[0], STMD[0].n), (STMD[1], STMD[1].n), (STTRN, STTRN.n)))
            with nc.Block() as blk:
                _replay(blk, w)

    return nc


def _host_inputs(inputs):
    x0 = np.ascontiguousarray(np.asarray(inputs["L0_attn_hidden_states"], np.float32))
    T = lambda m: np.ascontiguousarray(np.asarray(m, np.float32).T)
    Wq, Wk, Wv, Wo = (inputs["L0_attn_Wq"], inputs["L0_attn_Wk"],
                      inputs["L0_attn_Wv"], inputs["L0_attn_Wo"])
    Wgs, Wgw = inputs["L0_attn_Wgslc"], inputs["L0_attn_Wgswa"]
    Wgu, Wd = inputs["L0_ffn_w_gate_up"], inputs["L0_ffn_w_down"]
    cc = np.asarray(inputs["L0_c"], np.float32)
    wmod = np.ascontiguousarray(np.asarray(inputs["L0_wmod"], np.float32))

    def sbord(wT):
        R, N = wT.shape
        return np.ascontiguousarray(
            wT.reshape(R // 128, 128, N).transpose(1, 0, 2).reshape(128, -1))

    wqT = T(Wq)
    wq_p = np.stack([sbord(wqT[:, 128 * h:128 * (h + 1)]) for h in range(H)])
    woT = T(Wo)
    wo_p = np.stack([sbord(woT[:, 128 * c:128 * (c + 1)]) for c in range(NCH)])
    wguT = T(Wgu)
    wgu_pr = np.stack([
        sbord(np.concatenate([wguT[:, 512 * j:512 * (j + 1)],
                              wguT[:, FF + 512 * j:FF + 512 * (j + 1)]], axis=1)
              .reshape(C, 1024)) for j in range(16)])
    wdT = T(Wd)
    wd_p = np.stack([sbord(wdT[:, 128 * c:128 * (c + 1)]) for c in range(NCH)])
    wmod_pp = np.stack([sbord(wmod[:, 256 * j:256 * (j + 1)]) for j in range(48)])

    inv_freq = (1.0 / 10000.0 ** (np.arange(0, D, 2, dtype=np.float32) / D)).astype(np.float32)
    t = np.arange(S, dtype=np.float32)
    fr = np.outer(t, inv_freq).astype(np.float32)
    emb = np.concatenate([fr, fr], -1)
    cosT = np.ascontiguousarray(np.cos(emb).astype(np.float32).T)
    sinT = np.ascontiguousarray(np.sin(emb).astype(np.float32).T)

    identb = np.eye(128, dtype=np.float32)
    rotTc = np.zeros((128, 128), np.float32)
    for d_ in range(64):
        rotTc[d_ + 64, d_] = -1.0
        rotTc[d_, d_ + 64] = 1.0
    oneh = np.zeros((128, 256), np.float32)
    for h in range(16):
        oneh[:, 16 * h + h] = 1.0
    khot = np.zeros((128, 64), np.float32)
    for g in range(4):
        khot[:, 16 * g + 4 * g:16 * g + 4 * g + 4] = 1.0
    rsel8f = np.zeros((8, 1024), np.float32)
    for m in range(8):
        rsel8f[m, 128 * m:128 * (m + 1)] = 1.0
    row16 = np.zeros((16, 128), np.float32)
    row16[0] = 1.0
    row1 = np.ones((1, 128), np.float32)

    shared = dict(
        wkT=sbord(T(Wk)), wvT=sbord(T(Wv)),
        wgT=sbord(np.ascontiguousarray(np.concatenate([T(Wgs), T(Wgw)], axis=1))),
        wq_p=wq_p, wo_p=wo_p, wmod=wmod_pp, wgu_pr=wgu_pr, wd_p=wd_p,
        cosF=cosT, sinF=sinT,
        identb=identb.astype(BF), identf=identb.astype(np.float32),
        rotTc=rotTc.astype(BF), oneh=oneh.astype(BF), khot=khot.astype(BF),
        rsel8f=rsel8f, row16=row16.astype(BF), row1b=row1.astype(BF),
        row1f=row1, i16b=np.eye(16, dtype=np.float32).astype(BF),
        onesf=np.ones((128, 1), np.float32),
    )

    per_core = []
    for core in range(8):
        cls, b = core // 4, core % 4
        qcs = QCS[cls]
        rows = np.concatenate([np.arange(512 * q, 512 * (q + 1)) for q in qcs])
        xb = x0[b]
        m = np.zeros((SLOTS, 8, 128, 512), np.float32)
        for si, q in enumerate(qcs):
            for j in range(8):
                tkb = PAD[si] - 8 + j
                tq = 512 * q + np.arange(512)[None, :]
                tk = 128 * tkb + np.arange(128)[:, None]
                m[si, j] = (tq >= tk).astype(np.float32)
        per_core.append(dict(
            x_kv=np.ascontiguousarray(xb),
            x_own=np.ascontiguousarray(xb[rows]),
            xT_own=np.ascontiguousarray(xb[rows].T),
            cosO=np.ascontiguousarray(cosT[:, rows]),
            sinO=np.ascontiguousarray(sinT[:, rows]),
            cvec=np.ascontiguousarray(cc[b:b + 1]),
            masks=np.ascontiguousarray(m.astype(BF)),
            **shared,
        ))
    return per_core


def kernel(**inputs):
    per_core = _host_inputs(inputs)
    nc = build()
    res = run_bass_kernel_spmd(nc, per_core, core_ids=list(range(8)))
    out = np.empty((B, S, C), np.float32)
    for core in range(8):
        cls, b = core // 4, core % 4
        qcs = QCS[cls]
        o = np.ascontiguousarray(np.asarray(res.results[core]["outT"]).T)
        for si, q in enumerate(qcs):
            out[b, 512 * q:512 * (q + 1)] = o[512 * si:512 * (si + 1)]
    return out
